# version 01
# baseline (speedup 1.0000x reference)
import numpy as np
import ml_dtypes
import concourse.bass as bass
import concourse.mybir as mybir
from concourse.bass_utils import run_bass_kernel_spmd
from contextlib import ExitStack

F32 = mybir.dt.float32
BF16 = mybir.dt.bfloat16
AF = mybir.ActivationFunctionType
ALU = mybir.AluOpType
AX = mybir.AxisListType
BF = ml_dtypes.bfloat16

NCORES = 8
B = 2
S = 8192
D = 1024
TOWN = S // NCORES
H = 16
HD = 64


COMPUTE_ENGINES = ("t", "v", "s", "p")
DMA_QUEUES = ("sd", "gd")


class Prog:

    def __init__(self, nc):
        self.nc = nc
        self._sem_ctx = ExitStack()
        self.sems = {}
        for name in COMPUTE_ENGINES + DMA_QUEUES:
            self.sems[name] = self._sem_ctx.enter_context(nc.semaphore(f"sem_{name}"))
        self.count = {name: 0 for name in COMPUTE_ENGINES + DMA_QUEUES}
        self.stream = {name: [] for name in COMPUTE_ENGINES + DMA_QUEUES}
        self.res = {}
        self.water = {e: {s: 0 for s in COMPUTE_ENGINES + DMA_QUEUES}
                      for e in COMPUTE_ENGINES + DMA_QUEUES}
        self.open_group = {q: None for q in DMA_QUEUES}
        self.next_gid = 0
        self.barrier = {}

    def _close_group(self, q):
        gid = self.open_group[q]
        if gid is None:
            return
        total = self.count[q]
        self.barrier[gid] = total
        self.stream[q].append(([(q, total)], None, 0))
        self.water[q][q] = total
        self.open_group[q] = None

    def _resolve(self, ticket):
        semname, val = ticket
        if semname in DMA_QUEUES and isinstance(val, tuple):
            gid = val[1]
            if self.barrier.get(gid) is None:
                self._close_group(semname)
            return (semname, self.barrier[gid])
        return ticket

    def _dep_waits(self, eng, reads, writes):
        need = {}
        def add(t):
            semname, val = self._resolve(t)
            if val > need.get(semname, 0):
                need[semname] = val
        for k in reads:
            r = self.res.get(k)
            if r and r["w"]:
                add(r["w"])
        for k in writes:
            r = self.res.get(k)
            if r:
                if r["w"]:
                    add(r["w"])
                for rd in r["r"]:
                    add(rd)
        waits = []
        for semname, val in need.items():
            if self.water[eng][semname] >= val:
                continue
            self.water[eng][semname] = val
            waits.append((semname, val))
        return waits

    def op(self, eng, fn, reads=(), writes=()):
        assert eng in COMPUTE_ENGINES
        waits = self._dep_waits(eng, reads, writes)
        self.count[eng] += 1
        ticket = (eng, self.count[eng])
        for k in writes:
            self.res[k] = {"w": ticket, "r": []}
        for k in reads:
            self.res.setdefault(k, {"w": None, "r": []})["r"].append(ticket)
        self.stream[eng].append((waits, fn, 1))
        return ticket

    def dma(self, queue, out_ap, in_ap, reads=(), writes=()):
        assert queue in DMA_QUEUES
        waits = self._dep_waits(queue, reads, writes)
        if self.open_group[queue] is None:
            self.open_group[queue] = self.next_gid
            self.next_gid += 1
        gid = self.open_group[queue]
        self.count[queue] += 16
        ticket = (queue, ("g", gid))
        for k in writes:
            self.res[k] = {"w": ticket, "r": []}
        for k in reads:
            self.res.setdefault(k, {"w": None, "r": []})["r"].append(ticket)
        def fn(eng, out_ap=out_ap, in_ap=in_ap):
            return eng.dma_start(out=out_ap, in_=in_ap)
        self.stream[queue].append((waits, fn, 16))
        return ticket

    def fence_all_stores(self, eng="v", fn=None):
        for q in DMA_QUEUES:
            self._close_group(q)
        waits = []
        for q in DMA_QUEUES:
            if self.count[q] > 0 and self.water[eng][q] < self.count[q]:
                waits.append((q, self.count[q]))
                self.water[eng][q] = self.count[q]
        self.count[eng] += 1
        self.stream[eng].append((waits, fn, 1))

    def emit(self):
        nc = self.nc
        sems = self.sems

        def make_body(names):
            def body(eng):
                for name in names:
                    for waits, fn, inc in self.stream[name]:
                        for semname, val in waits:
                            eng.wait_ge(sems[semname], val)
                        if fn is None:
                            continue
                        inst = fn(eng)
                        if inc:
                            inst.then_inc(sems[name], inc)
            return body

        with nc.Block() as blk:
            if self.stream["sd"]:
                blk.sync(make_body(["sd"]))
            assert not (self.stream["p"] and self.stream["gd"])
            if self.stream["gd"]:
                blk.gpsimd(make_body(["gd"]))
            elif self.stream["p"]:
                blk.gpsimd(make_body(["p"]))
            if self.stream["t"]:
                blk.tensor(make_body(["t"]))
            if self.stream["v"]:
                blk.vector(make_body(["v"]))
            if self.stream["s"]:
                blk.scalar(make_body(["s"]))
        self._sem_ctx.close()



def host_rope_tables_A(t0, T):
    d = np.arange(128) % 64
    pair = d // 2
    exps = -np.arange(0, HD, 2)[: HD // 2].astype(np.float64) / HD
    freqs = 10000.0 ** exps
    t = np.arange(t0, t0 + T, dtype=np.float64)
    ang = t[None, :] * freqs[pair][:, None]
    C = np.cos(ang)
    Ssign = np.where((d % 2 == 0)[:, None], -np.sin(ang), np.sin(ang))
    return C.astype(np.float32), Ssign.astype(np.float32)

def host_pair_swap_perm():
    P = np.zeros((128, 128), np.float32)
    for i in range(128):
        P[i ^ 1, i] = 1.0
    return P

def host_identity():
    return np.eye(128, dtype=np.float32)



def build_A1(stage=4):
    nc = bass.Bass("TRN2", target_bir_lowering=False, debug=False)
    T = TOWN
    NT = T // 512
    KT = T // 128
    xT = nc.dram_tensor("xT", [B, D, T], F32, kind="ExternalInput").ap()
    wqT = nc.dram_tensor("wqT", [D, D], F32, kind="ExternalInput").ap()
    wkT = nc.dram_tensor("wkT", [D, D], F32, kind="ExternalInput").ap()
    wvT = nc.dram_tensor("wvT", [D, D], F32, kind="ExternalInput").ap()
    bq = nc.dram_tensor("bq", [128, 8], F32, kind="ExternalInput").ap()
    bk = nc.dram_tensor("bk", [128, 8], F32, kind="ExternalInput").ap()
    bv = nc.dram_tensor("bv", [128, 8], F32, kind="ExternalInput").ap()
    ropeC = nc.dram_tensor("ropeC", [128, T], F32, kind="ExternalInput").ap()
    ropeS = nc.dram_tensor("ropeS", [128, T], F32, kind="ExternalInput").ap()
    pswap = nc.dram_tensor("pswap", [128, 128], F32, kind="ExternalInput").ap()
    identf = nc.dram_tensor("identf", [128, 128], F32, kind="ExternalInput").ap()

    qT = nc.dram_tensor("qT", [B, D, T], BF16, kind="ExternalOutput").ap()
    kT = nc.dram_tensor("kT", [B, D, T], BF16, kind="ExternalOutput").ap()
    vtok = nc.dram_tensor("vtok", [B, H, 128, KT, 65], BF16, kind="ExternalOutput").ap()
    nrm = nc.dram_tensor("nrm", [2, 2 * B * 8], F32, kind="ExternalOutput").ap()

    ctx = ExitStack()
    sb = lambda n, s, d=F32: ctx.enter_context(nc.sbuf_tensor(n, s, d))
    pt = lambda n, s, d=F32: ctx.enter_context(nc.psum_tensor(n, s, d))

    xb = sb("xb", [128, B * 8 * T], BF16)
    xn = sb("xn", [128, B * 8 * T], BF16)
    w_one = sb("w_one", [128, 8 * D], BF16)
    w_sb = {pn: w_one for pn in ("q", "k", "v")}
    b_sb = {pn: sb(f"b_{pn}", [128, 8], F32) for pn in ("q", "k", "v")}
    Csb = sb("Csb", [128, T], F32)
    Ssb = sb("Ssb", [128, T], F32)
    psw = sb("psw", [128, 128], BF16)
    idn = sb("idn", [128, 128], BF16)
    ones_red = sb("ones_red", [128, 1], BF16)
    ones_bc = sb("ones_bc", [1, 128], F32)
    ind2 = sb("ind2", [128, 2], BF16)
    ro = [sb(f"ro{i}", [128, 512], BF16) for i in range(2)]
    vt_sb = sb("vt_sb", [128, B * KT * H * 65], BF16)
    pre = sb("pre", [128, 512], BF16)
    sq = sb("sq", [128, 512], BF16)
    t1 = sb("t1", [128, 512], F32)
    u1 = sb("u1", [128, 512], F32)
    mu = sb("mu", [1, B * T])
    musq = sb("musq", [1, B * T])
    varr = sb("varr", [1, B * T])
    sd_ = sb("sd_", [1, B * T])
    rstd = sb("rstd", [1, B * T])
    nmr = sb("nmr", [1, B * T])
    nmax = sb("nmax", [2, 2 * B * 8 * NT])
    nmax2 = sb("nmax2", [2, 2 * B * 8])
    tmpf = sb("tmpf", [128, 512], F32)
    epsc = sb("epsc", [1, 1], F32)

    stats_x = pt("stats_x", [1, 512])
    stats_xq = pt("stats_xq", [1, 512])
    Aps = pt("Aps", [128, 512])
    Bps = pt("Bps", [128, 512])
    proj = pt("proj", [128, 512])
    rot = pt("rot", [128, 512])
    qn_ps = pt("qn_ps", [2, 512])
    vt_ps = pt("vt_ps", [128, 512], BF16)

    p = Prog(nc)

    p.op("v", lambda v: v.memset(ones_red[:], 1.0), writes=["ones"])
    p.op("v", lambda v: v.memset(epsc[:], 1e-5), writes=["epsc"])
    p.op("v", lambda v: v.memset(ones_bc[:], 1.0), writes=["onesb"])
    p.op("v", lambda v: v.memset(ind2[:, :], 0.0), writes=["ind2"])
    p.op("v", lambda v: v.memset(ind2[0:64, 0:1], 1.0), reads=["ind2"], writes=["ind2"])
    p.op("v", lambda v: v.memset(ind2[64:128, 1:2], 1.0), reads=["ind2"], writes=["ind2"])
    vt_view = vt_sb[:].rearrange("p (b k h e) -> p b k h e", b=B, k=KT, h=H)
    p.op("v", lambda v: v.memset(vt_view[:, :, :, :, 64:65], 1.0), writes=["vtones"])

    for bb in range(B):
        for c in range(8):
            p.dma("gd", xb[:, (bb * 8 + c) * T:(bb * 8 + c + 1) * T],
                  xT[bb, c * 128:(c + 1) * 128, :], writes=[f"xb{bb}_{c}"])
    for nm, ap_b in (("q", bq), ("k", bk), ("v", bv)):
        p.dma("sd", b_sb[nm][:], ap_b, writes=[f"b{nm}"])
    p.dma("sd", Csb[:], ropeC, writes=["C"])
    p.dma("sd", Ssb[:], ropeS, writes=["S"])
    p.dma("gd", psw[:], pswap, writes=["psw"])
    p.dma("gd", idn[:], identf, writes=["idn"])

    for bb in range(B):
        for nt in range(NT):
            for c in range(8):
                col = (bb * 8 + c) * T + nt * 512
                xkey = f"xb{bb}_{c}"
                p.op("s", lambda s, col=col: s.activation(
                    sq[:], xb[:, col:col + 512], AF.Square),
                    reads=[xkey], writes=["sqtile"])
                p.op("t", lambda t, col=col, c=c: t.matmul(
                    stats_x[:], ones_red[:], xb[:, col:col + 512],
                    start=(c == 0), stop=(c == 7)),
                    reads=[xkey, "ones"], writes=["stats_x"])
                p.op("t", lambda t, c=c: t.matmul(
                    stats_xq[:], ones_red[:], sq[:],
                    start=(c == 0), stop=(c == 7)),
                    reads=["sqtile", "ones"], writes=["stats_xq"])
            scol = bb * T + nt * 512
            p.op("s", lambda s, scol=scol: s.activation(
                mu[:, scol:scol + 512], stats_x[:], AF.Copy, scale=1.0 / D),
                reads=["stats_x"], writes=["mu"])
            p.op("s", lambda s, scol=scol: s.activation(
                musq[:, scol:scol + 512], mu[:, scol:scol + 512], AF.Square),
                reads=["mu"], writes=["musq"])
            p.op("v", lambda v, scol=scol: v.scalar_tensor_tensor(
                varr[:, scol:scol + 512], stats_xq[:], 1.0 / D,
                musq[:, scol:scol + 512], ALU.mult, ALU.subtract),
                reads=["stats_xq", "musq"], writes=["varr"])
            p.op("s", lambda s, scol=scol: s.activation(
                sd_[:, scol:scol + 512], varr[:, scol:scol + 512], AF.Sqrt,
                bias=epsc[0:1, 0:1]),
                reads=["varr", "epsc"], writes=["sd"])
            p.op("v", lambda v, scol=scol: v.reciprocal(
                rstd[:, scol:scol + 512], sd_[:, scol:scol + 512]),
                reads=["sd"], writes=["rstd"])
            p.op("v", lambda v, scol=scol: v.scalar_tensor_tensor(
                nmr[:, scol:scol + 512], mu[:, scol:scol + 512], -1.0,
                rstd[:, scol:scol + 512], ALU.mult, ALU.mult),
                reads=["mu", "rstd"], writes=["nmr"])
            p.op("t", lambda t, scol=scol: t.matmul(
                Aps[:], ones_bc[:], rstd[:, scol:scol + 512], start=True, stop=True),
                reads=["rstd", "onesb"], writes=["Aps"])
            p.op("t", lambda t, scol=scol: t.matmul(
                Bps[:], ones_bc[:], nmr[:, scol:scol + 512], start=True, stop=True),
                reads=["nmr", "onesb"], writes=["Bps"])
            for c in range(8):
                col = (bb * 8 + c) * T + nt * 512
                p.op("v", lambda v, col=col: v.tensor_mul(
                    tmpf[:], xb[:, col:col + 512], Aps[:]),
                    reads=[f"xb{bb}_{c}", "Aps"], writes=["tmpf"])
                p.op("v", lambda v, col=col: v.tensor_add(
                    xn[:, col:col + 512], tmpf[:], Bps[:]),
                    reads=["tmpf", "Bps"], writes=[f"xn{bb}_{c}_{nt}"])

    if stage == 1:
        for bb in range(B):
            for oc in range(8):
                p.dma("sd", qT[bb, oc * 128:(oc + 1) * 128, :],
                      xn[:, (bb * 8 + oc) * T:(bb * 8 + oc + 1) * T],
                      reads=[f"xn{bb}_{oc}_{nt}" for nt in range(NT)])
        p.fence_all_stores("v", lambda v: v.tensor_copy(tmpf[:, 0:1], tmpf[:, 0:1]))
        p.emit()
        ctx.close()
        return nc

    wmap = {"q": wqT, "k": wkT, "v": wvT}
    projlist = {2: ("q",), 3: ("q", "k"), 4: ("q", "k", "v")}[stage]
    for pn in projlist:
        pidx = {"q": 0, "k": 1}.get(pn)
        p.dma("gd", w_one[:].rearrange("p (k m) -> p k m", k=8),
              wmap[pn].rearrange("(k p) m -> p k m", p=128), writes=["w_one"])
        for bb in range(B):
            for oc in range(8):
                for nt in range(NT):
                    xn_keys = [f"xn{bb}_{kc}_{nt}" for kc in range(8)]
                    def mmgroup(t, bb=bb, oc=oc, nt=nt, pn=pn):
                        last = None
                        for kc in range(8):
                            col = (bb * 8 + kc) * T + nt * 512
                            last = t.matmul(
                                proj[:],
                                w_sb[pn][:, kc * D + oc * 128: kc * D + (oc + 1) * 128],
                                xn[:, col:col + 512],
                                start=(kc == 0), stop=(kc == 7))
                        return last
                    p.op("t", mmgroup, reads=xn_keys + ["w_one"], writes=["proj"])
                    p.op("s", lambda s, pn=pn, oc=oc: s.activation(
                        pre[:], proj[:], AF.Identity, bias=b_sb[pn][:, oc:oc + 1]),
                        reads=["proj", f"b{pn}"], writes=["pre"])
                    if pn == "v":
                        def trgroup(t):
                            last = None
                            for q4 in range(4):
                                last = t.transpose(
                                    vt_ps[:, q4 * 128:(q4 + 1) * 128],
                                    pre[:, q4 * 128:(q4 + 1) * 128], idn[:])
                            return last
                        p.op("t", trgroup, reads=["pre", "idn"], writes=["vt_ps"])
                        for q4 in range(4):
                            kt = (nt * 512 + q4 * 128) // 128
                            for par in range(2):
                                h = oc * 2 + par
                                p.op("v", lambda v, q4=q4, par=par, bb=bb, kt=kt, h=h:
                                     v.tensor_copy(
                                         vt_view[:, bb, kt, h, 0:64],
                                         vt_ps[:, q4 * 128 + par * 64:
                                               q4 * 128 + par * 64 + 64]),
                                     reads=["vt_ps"], writes=[f"vt{bb}_{kt}_{h}"])
                    else:
                        p.op("s", lambda s: s.activation(sq[:], pre[:], AF.Square),
                             reads=["pre"], writes=["sqtile"])
                        p.op("t", lambda t: t.matmul(
                            qn_ps[:], ind2[:], sq[:], start=True, stop=True),
                            reads=["sqtile", "ind2"], writes=["qn_ps"])
                        ncol = ((pidx * B + bb) * 8 + oc) * NT + nt
                        p.op("v", lambda v, ncol=ncol: v.reduce_max(
                            nmax[:, ncol:ncol + 1], qn_ps[:], axis=AX.X),
                            reads=["qn_ps", "nmax"], writes=["nmax"])
                        p.op("t", lambda t: t.matmul(
                            rot[:], psw[:], pre[:], start=True, stop=True),
                            reads=["pre", "psw"], writes=["rot"])
                        tcol = nt * 512
                        p.op("v", lambda v, tcol=tcol: v.tensor_mul(
                            t1[:], pre[:], Csb[:, tcol:tcol + 512]),
                            reads=["pre", "C"], writes=["t1"])
                        p.op("v", lambda v, tcol=tcol: v.tensor_mul(
                            u1[:], rot[:], Ssb[:, tcol:tcol + 512]),
                            reads=["rot", "S"], writes=["u1"])
                        ri = (oc * NT + nt) % 2
                        okey = f"{pn}o{bb}_{oc}_{nt}"
                        p.op("v", lambda v, ri=ri: v.tensor_add(
                            ro[ri][:], t1[:], u1[:]),
                            reads=["t1", "u1"], writes=[okey, f"ro{ri}"])
                        dram = qT if pn == "q" else kT
                        p.dma("sd", dram[bb, oc * 128:(oc + 1) * 128,
                                         nt * 512:(nt + 1) * 512],
                              ro[ri][:], reads=[okey, f"ro{ri}"])

    p.op("v", lambda v: v.reduce_max(
        nmax2[:], nmax[:].rearrange("p (g n) -> p g n", n=NT), axis=AX.X),
        reads=["nmax"], writes=["nmax2"])
    if stage < 4:
        p.dma("sd", nrm, nmax2[:], reads=["nmax2"])
        p.fence_all_stores("v", lambda v: v.tensor_copy(tmpf[:, 0:1], tmpf[:, 0:1]))
        p.emit()
        ctx.close()
        return nc

    for bb in range(B):
        for h in range(H):
            vt_reads = [f"vt{bb}_{kt}_{h}" for kt in range(KT)] + ["vtones"]
            p.dma("sd", vtok[bb, h], vt_view[:, bb, :, h, :], reads=vt_reads)
    p.dma("sd", nrm, nmax2[:], reads=["nmax2"])

    p.fence_all_stores("v", lambda v: v.tensor_copy(tmpf[:, 0:1], tmpf[:, 0:1]))
    p.emit()
    ctx.close()
    return nc



_cache = {}

def _get(name, fn):
    if name not in _cache:
        _cache[name] = fn()
    return _cache[name]


def host_prep_A1(inputs):
    x = inputs["L0_attn_x"]
    xTfull = np.ascontiguousarray(x.transpose(0, 2, 1))
    wqT = np.ascontiguousarray(inputs["L0_attn_Wq"].T)
    wkT = np.ascontiguousarray(inputs["L0_attn_Wk"].T)
    wvT = np.ascontiguousarray(inputs["L0_attn_Wv"].T)
    bq = np.ascontiguousarray(inputs["L0_attn_bq"].reshape(8, 128).T)
    bk = np.ascontiguousarray(inputs["L0_attn_bk"].reshape(8, 128).T)
    bv = np.ascontiguousarray(inputs["L0_attn_bv"].reshape(8, 128).T)
    psw = host_pair_swap_perm()
    idn = host_identity()
    in_maps = []
    for c in range(NCORES):
        t0 = c * TOWN
        C, Ssign = host_rope_tables_A(t0, TOWN)
        in_maps.append({
            "xT": np.ascontiguousarray(xTfull[:, :, t0:t0 + TOWN]),
            "wqT": wqT, "wkT": wkT, "wvT": wvT,
            "bq": bq, "bk": bk, "bv": bv,
            "ropeC": C, "ropeS": Ssign, "pswap": psw, "identf": idn,
        })
    return in_maps






def build_A2(nheads=H):
    nc = bass.Bass("TRN2", target_bir_lowering=False, debug=False)
    T = TOWN
    SKT = S // 128
    qTd = nc.dram_tensor("qTd", [B, D, T], BF16, kind="ExternalInput").ap()
    kTd = nc.dram_tensor("kTd", [B, D, S], BF16, kind="ExternalInput").ap()
    vAd = nc.dram_tensor("vAd", [B, H, 128, SKT, 65], BF16, kind="ExternalInput").ap()
    nrmO = nc.dram_tensor("nrmO", [2, 2 * B * 8], F32, kind="ExternalInput").ap()
    nrmA = nc.dram_tensor("nrmA", [NCORES, 2, 2 * B * 8], F32, kind="ExternalInput").ap()
    sel2 = nc.dram_tensor("sel2", [2, 2], F32, kind="ExternalInput").ap()
    attnO = nc.dram_tensor("attnO", [B, D, T], BF16, kind="ExternalOutput").ap()

    ctx = ExitStack()
    sb = lambda n, s, d=F32: ctx.enter_context(nc.sbuf_tensor(n, s, d))
    pt = lambda n, s, d=F32: ctx.enter_context(nc.psum_tensor(n, s, d))

    k_buf = [sb(f"k_buf{i}", [64, S], BF16) for i in range(2)]
    v_buf = [sb(f"v_buf{i}", [128, SKT * 65], BF16) for i in range(2)]
    q_buf = [sb(f"q_buf{i}", [64, T], BF16) for i in range(2)]
    p_sb = [sb(f"p_sb{i}", [128, 2048], BF16) for i in range(2)]
    rb_sb = sb("rb_sb", [64, 512])
    o_sb = [sb(f"o_sb{i}", [64, 512], BF16) for i in range(2)]
    r1 = sb("r1", [1, 512])
    nrmo_sb = sb("nrmo_sb", [2, 2 * B * 8])
    nrma_sb = sb("nrma_sb", [2, NCORES * 2 * B * 8])
    kall = sb("kall", [2, 2 * B * 8])
    qsr = sb("qsr", [2, B * 8])
    ksr = sb("ksr", [2, B * 8])
    prodc = sb("prodc", [2, B * 8])
    negc = sb("negc", [2, B * 8])
    negc_bf = sb("negc_bf", [2, B * 8], BF16)
    negCb = sb("negCb", [128, 2 * B * 8])
    sel = sb("sel", [2, 2], BF16)
    ones_bc = sb("ones_bc", [1, 128])
    rowf = sb("rowf", [1, B * 8])

    score_ps = pt("score_ps", [128, 2048])
    av_ps = pt("av_ps", [65, 512])
    rb_ps = pt("rb_ps", [64, 512])
    cb_ps = pt("cb_ps", [128, B * 8])
    row_ps = pt("row_ps", [1, B * 8])

    p = Prog(nc)

    p.dma("sd", nrmo_sb[:], nrmO, writes=["nrmo"])
    p.dma("sd", nrma_sb[:].rearrange("p (c g) -> p c g", c=NCORES),
          nrmA.rearrange("c p g -> p c g"), writes=["nrma"])
    p.op("v", lambda v: v.memset(ones_bc[:], 1.0), writes=["onesbc"])
    p.dma("gd", sel[:], sel2, writes=["sel"])
    p.op("v", lambda v: v.reduce_max(
        kall[:], nrma_sb[:].rearrange("p (c g) -> p g c", c=NCORES), axis=AX.X),
        reads=["nrma"], writes=["kall"])
    p.op("s", lambda s: s.activation(qsr[:], nrmo_sb[:, 0:B * 8], AF.Sqrt),
         reads=["nrmo"], writes=["qsr"])
    p.op("s", lambda s: s.activation(ksr[:], kall[:, B * 8:2 * B * 8], AF.Sqrt),
         reads=["kall"], writes=["ksr"])
    p.op("v", lambda v: v.tensor_mul(prodc[:], qsr[:], ksr[:]),
         reads=["qsr", "ksr"], writes=["prodc"])
    p.op("s", lambda s: s.activation(negc[:], prodc[:], AF.Copy, scale=-1.0 / 32.0),
         reads=["prodc"], writes=["negc"])
    p.op("v", lambda v: v.tensor_copy(negc_bf[:], negc[:]), reads=["negc"],
         writes=["negcbf"])
    for par in range(2):
        p.op("t", lambda t, par=par: t.matmul(
            row_ps[:], sel[:, par:par + 1], negc_bf[:], start=True, stop=True),
            reads=["sel", "negcbf"], writes=["row_ps"])
        p.op("v", lambda v: v.tensor_copy(rowf[:], row_ps[:]),
             reads=["row_ps"], writes=["rowf"])
        p.op("t", lambda t: t.matmul(
            cb_ps[:], ones_bc[:], rowf[:], start=True, stop=True),
            reads=["onesbc", "rowf"], writes=["cb_ps"])
        p.op("v", lambda v, par=par: v.tensor_copy(
            negCb[:, par * B * 8:(par + 1) * B * 8], cb_ps[:]),
            reads=["cb_ps"], writes=["negCb"])

    for bb in range(B):
        for h in range(nheads):
            cur = (bb * nheads + h) % 2
            kkey, vkey, qkey = f"kb{cur}", f"vb{cur}", f"qb{cur}"
            p.dma("gd", k_buf[cur][:], kTd[bb, h * 64:(h + 1) * 64, :], writes=[kkey])
            p.dma("gd", v_buf[cur][:].rearrange("p (k e) -> p k e", k=SKT),
                  vAd[bb, h], writes=[vkey])
            p.dma("gd", q_buf[cur][:], qTd[bb, h * 64:(h + 1) * 64, :], writes=[qkey])
            oc, par = h // 2, h % 2
            ccol = par * B * 8 + bb * 8 + oc
            for qt in range(T // 512):
                for kg in range(SKT // 4):
                    pi = kg % 2
                    for kc4 in range(4):
                        kt = kg * 4 + kc4
                        p.op("t", lambda t, cur=cur, kt=kt, qt=qt, kc4=kc4: t.matmul(
                            score_ps[:, kc4 * 512:(kc4 + 1) * 512],
                            k_buf[cur][:, kt * 128:(kt + 1) * 128],
                            q_buf[cur][:, qt * 512:(qt + 1) * 512],
                            start=True, stop=True),
                            reads=[kkey, qkey], writes=[f"sc{kc4}"])
                    p.op("s", lambda s, pi=pi, ccol=ccol: s.activation(
                        p_sb[pi][:], score_ps[:], AF.Exp,
                        bias=negCb[:, ccol:ccol + 1], scale=1.0 / 32.0),
                        reads=["sc0", "sc1", "sc2", "sc3", "negCb"],
                        writes=[f"pexp{pi}"])
                    for kc4 in range(4):
                        kt = kg * 4 + kc4
                        p.op("t", lambda t, cur=cur, kt=kt, pi=pi, kc4=kc4: t.matmul(
                            av_ps[:],
                            v_buf[cur][:].rearrange("p (k e) -> p k e", k=SKT)[:, kt, :],
                            p_sb[pi][:, kc4 * 512:(kc4 + 1) * 512],
                            start=(kt == 0), stop=(kt == SKT - 1)),
                            reads=[vkey, f"pexp{pi}"], writes=["av_ps"])
                p.op("v", lambda v: v.reciprocal(r1[:], av_ps[64:65, :]),
                     reads=["av_ps"], writes=["r1"])
                p.op("t", lambda t: t.matmul(
                    rb_ps[:], ones_bc[0:1, 0:64], r1[:], start=True, stop=True),
                    reads=["onesbc", "r1"], writes=["rb_ps"])
                p.op("v", lambda v: v.tensor_copy(rb_sb[:], rb_ps[:]),
                     reads=["rb_ps"], writes=["rb_sb"])
                oi = qt % 2
                p.op("v", lambda v, oi=oi: v.tensor_mul(
                    o_sb[oi][:], av_ps[0:64, :], rb_sb[:]),
                    reads=["av_ps", "rb_sb"], writes=[f"o{oi}"])
                p.dma("sd", attnO[bb, h * 64:(h + 1) * 64, qt * 512:(qt + 1) * 512],
                      o_sb[oi][:], reads=[f"o{oi}"])

    p.fence_all_stores("v", lambda v: v.tensor_copy(r1[0:1, 0:1], r1[0:1, 0:1]))
    p.emit()
    ctx.close()
    return nc



def emit_ln_plain(p, r, bb_range, NT, T, xb, xkey, xn, xnkey):
    for bb in bb_range:
        for nt in range(NT):
            for c in range(8):
                col = (bb * 8 + c) * T + nt * 512
                p.op("s", lambda s, col=col: s.activation(
                    r["sq"][:], xb[:, col:col + 512], AF.Square),
                    reads=[xkey(bb, c)], writes=["sqtile"])
                p.op("t", lambda t, col=col, c=c: t.matmul(
                    r["stats_x"][:], r["ones_red"][:], xb[:, col:col + 512],
                    start=(c == 0), stop=(c == 7)),
                    reads=[xkey(bb, c), "ones"], writes=["stats_x"])
                p.op("t", lambda t, c=c: t.matmul(
                    r["stats_xq"][:], r["ones_red"][:], r["sq"][:],
                    start=(c == 0), stop=(c == 7)),
                    reads=["sqtile", "ones"], writes=["stats_xq"])
            sc = 0
            p.op("s", lambda s: s.activation(
                r["mu"][:], r["stats_x"][:], AF.Copy, scale=1.0 / D),
                reads=["stats_x"], writes=["mu"])
            p.op("s", lambda s: s.activation(r["musq"][:], r["mu"][:], AF.Square),
                 reads=["mu"], writes=["musq"])
            p.op("v", lambda v: v.scalar_tensor_tensor(
                r["varr"][:], r["stats_xq"][:], 1.0 / D, r["musq"][:],
                ALU.mult, ALU.subtract),
                reads=["stats_xq", "musq"], writes=["varr"])
            p.op("s", lambda s: s.activation(
                r["sd_"][:], r["varr"][:], AF.Sqrt, bias=r["epsc"][0:1, 0:1]),
                reads=["varr", "epsc"], writes=["sd"])
            p.op("v", lambda v: v.reciprocal(r["rstd"][:], r["sd_"][:]),
                 reads=["sd"], writes=["rstd"])
            p.op("v", lambda v: v.scalar_tensor_tensor(
                r["nmr"][:], r["mu"][:], -1.0, r["rstd"][:], ALU.mult, ALU.mult),
                reads=["mu", "rstd"], writes=["nmr"])
            p.op("t", lambda t: t.matmul(
                r["Aps"][:], r["ones_bc"][:], r["rstd"][:], start=True, stop=True),
                reads=["rstd", "onesb"], writes=["Aps"])
            p.op("t", lambda t: t.matmul(
                r["Bps"][:], r["ones_bc"][:], r["nmr"][:], start=True, stop=True),
                reads=["nmr", "onesb"], writes=["Bps"])
            for c in range(8):
                col = (bb * 8 + c) * T + nt * 512
                p.op("v", lambda v, col=col: v.tensor_mul(
                    r["tmpf"][:], xb[:, col:col + 512], r["Aps"][:]),
                    reads=[xkey(bb, c), "Aps"], writes=["tmpf"])
                p.op("v", lambda v, col=col: v.tensor_add(
                    xn[:, col:col + 512], r["tmpf"][:], r["Bps"][:]),
                    reads=["tmpf", "Bps"], writes=[xnkey(bb, c, nt)])


def alloc_ln_scratch(nc, ctx, p, maxvec=512):
    sb = lambda n, s, d=F32: ctx.enter_context(nc.sbuf_tensor(n, s, d))
    pt = lambda n, s, d=F32: ctx.enter_context(nc.psum_tensor(n, s, d))
    r = {
        "sq": sb("lnsq", [128, 512], BF16),
        "mu": sb("lnmu", [1, 512]), "musq": sb("lnmusq", [1, 512]),
        "varr": sb("lnvarr", [1, 512]), "sd_": sb("lnsd", [1, 512]),
        "rstd": sb("lnrstd", [1, 512]), "nmr": sb("lnnmr", [1, 512]),
        "tmpf": sb("lntmpf", [128, 512], F32),
        "ones_red": sb("ones_red", [128, 1], BF16),
        "ones_bc": sb("ones_bc", [1, 128], F32),
        "epsc": sb("epsc", [1, 1], F32),
        "stats_x": pt("stats_x", [1, 512]),
        "stats_xq": pt("stats_xq", [1, 512]),
        "Aps": pt("Aps", [128, 512]),
        "Bps": pt("Bps", [128, 512]),
    }
    p.op("v", lambda v: v.memset(r["ones_red"][:], 1.0), writes=["ones"])
    p.op("v", lambda v: v.memset(r["ones_bc"][:], 1.0), writes=["onesb"])
    p.op("v", lambda v: v.memset(r["epsc"][:], 1e-5), writes=["epsc"])
    return r



def build_A3():
    nc = bass.Bass("TRN2", target_bir_lowering=False, debug=False)
    T = TOWN
    NT = T // 512
    xT = nc.dram_tensor("xT", [B, D, T], F32, kind="ExternalInput").ap()
    aoT = nc.dram_tensor("aoT", [B, D, T], BF16, kind="ExternalInput").ap()
    woT = nc.dram_tensor("woT", [D, D], F32, kind="ExternalInput").ap()
    x1T = nc.dram_tensor("x1T", [B, D, T], F32, kind="ExternalOutput").ap()
    xx0T = nc.dram_tensor("xx0T", [B, D, T], F32, kind="ExternalOutput").ap()

    ctx = ExitStack()
    sb = lambda n, s, d=F32: ctx.enter_context(nc.sbuf_tensor(n, s, d))
    pt = lambda n, s, d=F32: ctx.enter_context(nc.psum_tensor(n, s, d))

    x_sb = sb("x_sb", [128, B * 8 * T], F32)
    ao = sb("ao", [128, B * 8 * T], BF16)
    w_one = sb("w_one", [128, 8 * D], BF16)
    x1b = sb("x1b", [128, B * 8 * T], BF16)
    xnout = [sb(f"xnout{i}", [128, 512], BF16) for i in range(2)]
    xnf = [sb(f"xnf{i}", [128, 512], F32) for i in range(2)]
    proj = pt("proj", [128, 512])

    p = Prog(nc)
    r = alloc_ln_scratch(nc, ctx, p)

    for bb in range(B):
        for c in range(8):
            p.dma("sd", x_sb[:, (bb * 8 + c) * T:(bb * 8 + c + 1) * T],
                  xT[bb, c * 128:(c + 1) * 128, :], writes=[f"x{bb}_{c}"])
            p.dma("gd", ao[:, (bb * 8 + c) * T:(bb * 8 + c + 1) * T],
                  aoT[bb, c * 128:(c + 1) * 128, :], writes=[f"ao{bb}_{c}"])
    p.dma("gd", w_one[:].rearrange("p (k m) -> p k m", k=8),
          woT.rearrange("(k p) m -> p k m", p=128), writes=["w_one"])

    for bb in range(B):
        for oc in range(8):
            for nt in range(NT):
                aokeys = [f"ao{bb}_{kc}" for kc in range(8)]
                def mmg(t, bb=bb, oc=oc, nt=nt):
                    last = None
                    for kc in range(8):
                        col = (bb * 8 + kc) * T + nt * 512
                        last = t.matmul(
                            proj[:],
                            w_one[:, kc * D + oc * 128: kc * D + (oc + 1) * 128],
                            ao[:, col:col + 512], start=(kc == 0), stop=(kc == 7))
                    return last
                p.op("t", mmg, reads=aokeys + ["w_one"], writes=["proj"])
                col = (bb * 8 + oc) * T + nt * 512
                p.op("v", lambda v, col=col: v.tensor_add(
                    x_sb[:, col:col + 512], x_sb[:, col:col + 512], proj[:]),
                    reads=[f"x{bb}_{oc}", "proj"], writes=[f"x{bb}_{oc}"])
                p.op("s", lambda s, col=col: s.activation(
                    x1b[:, col:col + 512], x_sb[:, col:col + 512], AF.Copy),
                    reads=[f"x{bb}_{oc}"], writes=[f"xb{bb}_{oc}"])
                p.dma("sd", x1T[bb, oc * 128:(oc + 1) * 128, nt * 512:(nt + 1) * 512],
                      x_sb[:, col:col + 512], reads=[f"x{bb}_{oc}"])

    for bb in range(B):
        for nt in range(NT):
            for c in range(8):
                col = (bb * 8 + c) * T + nt * 512
                p.op("s", lambda s, col=col: s.activation(
                    r["sq"][:], x1b[:, col:col + 512], AF.Square),
                    reads=[f"xb{bb}_{c}"], writes=["sqtile"])
                p.op("t", lambda t, col=col, c=c: t.matmul(
                    r["stats_x"][:], r["ones_red"][:], x1b[:, col:col + 512],
                    start=(c == 0), stop=(c == 7)),
                    reads=[f"xb{bb}_{c}", "ones"], writes=["stats_x"])
                p.op("t", lambda t, c=c: t.matmul(
                    r["stats_xq"][:], r["ones_red"][:], r["sq"][:],
                    start=(c == 0), stop=(c == 7)),
                    reads=["sqtile", "ones"], writes=["stats_xq"])
            p.op("s", lambda s: s.activation(
                r["mu"][:], r["stats_x"][:], AF.Copy, scale=1.0 / D),
                reads=["stats_x"], writes=["mu"])
            p.op("s", lambda s: s.activation(r["musq"][:], r["mu"][:], AF.Square),
                 reads=["mu"], writes=["musq"])
            p.op("v", lambda v: v.scalar_tensor_tensor(
                r["varr"][:], r["stats_xq"][:], 1.0 / D, r["musq"][:],
                ALU.mult, ALU.subtract),
                reads=["stats_xq", "musq"], writes=["varr"])
            p.op("s", lambda s: s.activation(
                r["sd_"][:], r["varr"][:], AF.Sqrt, bias=r["epsc"][0:1, 0:1]),
                reads=["varr", "epsc"], writes=["sd"])
            p.op("v", lambda v: v.reciprocal(r["rstd"][:], r["sd_"][:]),
                 reads=["sd"], writes=["rstd"])
            p.op("v", lambda v: v.scalar_tensor_tensor(
                r["nmr"][:], r["mu"][:], -1.0, r["rstd"][:], ALU.mult, ALU.mult),
                reads=["mu", "rstd"], writes=["nmr"])
            p.op("t", lambda t: t.matmul(
                r["Aps"][:], r["ones_bc"][:], r["rstd"][:], start=True, stop=True),
                reads=["rstd", "onesb"], writes=["Aps"])
            p.op("t", lambda t: t.matmul(
                r["Bps"][:], r["ones_bc"][:], r["nmr"][:], start=True, stop=True),
                reads=["nmr", "onesb"], writes=["Bps"])
            for c in range(8):
                col = (bb * 8 + c) * T + nt * 512
                xi = c % 2
                p.op("v", lambda v, col=col, xi=xi: v.tensor_mul(
                    r["tmpf"][:], x_sb[:, col:col + 512], r["Aps"][:]),
                    reads=[f"x{bb}_{c}", "Aps"], writes=["tmpf"])
                p.op("v", lambda v, col=col, xi=xi: v.tensor_add(
                    xnf[xi][:], r["tmpf"][:], r["Bps"][:]),
                    reads=["tmpf", "Bps"], writes=[f"xnf{xi}"])
                p.dma("sd", xx0T[bb, c * 128:(c + 1) * 128, nt * 512:(nt + 1) * 512],
                      xnf[xi][:], reads=[f"xnf{xi}"])

    p.fence_all_stores("v", lambda v: v.tensor_copy(r["mu"][0:1, 0:1], r["mu"][0:1, 0:1]))
    p.emit()
    ctx.close()
    return nc



def build_Battn():
    nc = bass.Bass("TRN2", target_bir_lowering=False, debug=False)
    TQ = 1280
    TH = 1281
    xxT = nc.dram_tensor("xxT", [B, D, TH], F32, kind="ExternalInput").ap()
    wqkv = nc.dram_tensor("wqkv", [D, 3 * D], F32, kind="ExternalInput").ap()
    woB = nc.dram_tensor("woB", [D, D], F32, kind="ExternalInput").ap()
    qs128 = nc.dram_tensor("qs128", [128, 1], F32, kind="ExternalInput").ap()
    qs128sw = nc.dram_tensor("qs128sw", [128, 1], F32, kind="ExternalInput").ap()
    ks128 = nc.dram_tensor("ks128", [128, 1], F32, kind="ExternalInput").ap()
    ks128sw = nc.dram_tensor("ks128sw", [128, 1], F32, kind="ExternalInput").ap()
    qsrow = nc.dram_tensor("qsrow", [1, 64], F32, kind="ExternalInput").ap()
    ksrow = nc.dram_tensor("ksrow", [1, 64], F32, kind="ExternalInput").ap()
    tabs = {nm: nc.dram_tensor(nm, [128, TQ], F32, kind="ExternalInput").ap()
            for nm in ("tabCq", "tabSq", "tabCk", "tabSk")}
    p32d = nc.dram_tensor("p32", [128, 128], F32, kind="ExternalInput").ap()
    ind2Td = nc.dram_tensor("ind2T", [2, 128], F32, kind="ExternalInput").ap()
    masksd = nc.dram_tensor("masksB", [2, 4, 128, 256], F32, kind="ExternalInput").ap()
    identd = nc.dram_tensor("identf", [128, 128], F32, kind="ExternalInput").ap()
    xxoT = nc.dram_tensor("xxoT", [B, D, TOWN], F32, kind="ExternalOutput").ap()

    ctx = ExitStack()
    sb = lambda n, s, d=F32: ctx.enter_context(nc.sbuf_tensor(n, s, d))
    pt = lambda n, s, d=F32: ctx.enter_context(nc.psum_tensor(n, s, d))

    w_sb = sb("w_sb", [128, 8 * 3 * D], BF16)
    wo_sb = sb("wo_sb", [128, 8 * D], BF16)
    tab_sb = {nm: sb(nm + "s", [128, TQ], F32) for nm in
              ("tabCq", "tabSq", "tabCk", "tabSk")}
    mask_sb = sb("mask_sb", [128, 2 * 4 * 256], BF16)
    p32s = sb("p32s", [128, 128], BF16)
    idn = sb("idn", [128, 128], BF16)
    ind2Ts = sb("ind2Ts", [2, 128], F32)
    ind2 = sb("ind2", [128, 2], BF16)
    qsc = {nm: sb(nm + "s", [128, 1], F32) for nm in
           ("qs128", "qs128sw", "ks128", "ks128sw")}
    qsr_ = sb("qsr_", [1, 64])
    ksr_ = sb("ksr_", [1, 64])
    ones_bc = sb("ones_bc", [1, 128], F32)
    epsl2 = sb("epsl2", [1, 1], F32)
    cmax = sb("cmax", [1, 2])
    Cb = sb("Cb", [128, 2])
    xxb = sb("xxb", [128, 8 * TH], BF16)
    qTb = sb("qTb", [128, 8 * TOWN], BF16)
    kTb = sb("kTb", [128, 8 * TQ], BF16)
    vtk = sb("vtk", [128, 10 * H * 65], BF16)
    pre = sb("pre", [128, 256], BF16)
    sq = sb("sq", [128, 256], BF16)
    t1 = sb("t1", [128, 256], F32)
    u1 = sb("u1", [128, 256], F32)
    t2 = sb("t2", [128, 256], F32)
    nrm2 = sb("nrm2", [2, 256])
    nrm = sb("nrm", [2, 256])
    rn_ = sb("rn_", [2, 256])
    sc_f = sb("sc_f", [128, 1024], F32)
    p_sb = sb("p_sb", [128, 1024], BF16)
    r1 = sb("r1", [1, 256])
    rb_sb = sb("rb_sb", [64, 256])
    o_asm = sb("o_asm", [128, 8 * 256], BF16)
    rxf = [sb(f"rxf{i}", [128, 256], F32) for i in range(2)]
    rxo = [sb(f"rxo{i}", [128, 256], F32) for i in range(2)]

    score_ps = pt("score_ps", [128, 1024])
    av_ps = pt("av_ps", [65, 256])
    util_ps = pt("util_ps", [128, 256])
    proj_ps = pt("proj_ps", [128, 256])
    qn_ps = pt("qn_ps", [2, 256])
    wo_ps = pt("wo_ps", [128, 256])
    vt_ps = pt("vt_ps", [128, 256], BF16)

    p = Prog(nc)

    p.op("v", lambda v: v.memset(ones_bc[:], 1.0), writes=["onesb"])
    p.op("v", lambda v: v.memset(epsl2[:], 1e-12), writes=["epsl2"])
    p.op("v", lambda v: v.memset(ind2[:, :], 0.0), writes=["ind2"])
    p.op("v", lambda v: v.memset(ind2[0:64, 0:1], 1.0), reads=["ind2"], writes=["ind2"])
    p.op("v", lambda v: v.memset(ind2[64:128, 1:2], 1.0), reads=["ind2"], writes=["ind2"])
    vt_view = vtk[:].rearrange("p (k h e) -> p k h e", k=10, h=H)
    p.op("v", lambda v: v.memset(vt_view[:, :, :, 64:65], 1.0), writes=["vtones"])
    p.dma("gd", w_sb[:].rearrange("p (k m) -> p k m", k=8),
          wqkv.rearrange("(k p) m -> p k m", p=128), writes=["w_sb"])
    p.dma("gd", wo_sb[:].rearrange("p (k m) -> p k m", k=8),
          woB.rearrange("(k p) m -> p k m", p=128), writes=["wo_sb"])
    for nm in tab_sb:
        p.dma("sd", tab_sb[nm][:], tabs[nm], writes=[nm])
    p.dma("gd", mask_sb[:].rearrange("p (s k m) -> p s k m", s=2, k=4),
          masksd.rearrange("s k p m -> p s k m"), writes=["mask"])
    p.dma("gd", p32s[:], p32d, writes=["p32"])
    p.dma("gd", idn[:], identd, writes=["idn"])
    p.dma("sd", ind2Ts[:], ind2Td, writes=["ind2T"])
    for nm in qsc:
        p.dma("sd", qsc[nm][:], {"qs128": qs128, "qs128sw": qs128sw,
                                 "ks128": ks128, "ks128sw": ks128sw}[nm],
              writes=[nm])
    p.dma("sd", qsr_[:], qsrow, writes=["qsr"])
    p.dma("sd", ksr_[:], ksrow, writes=["ksr"])

    qm = sb("qm", [1, 1]); km = sb("km", [1, 1]); pm = sb("pm", [1, 2])
    p.op("v", lambda v: v.reduce_max(qm[:], qsr_[:], axis=AX.X,
         apply_absolute_value=True), reads=["qsr"], writes=["qm"])
    p.op("v", lambda v: v.reduce_max(km[:], ksr_[:], axis=AX.X,
         apply_absolute_value=True), reads=["ksr"], writes=["km"])
    p.op("v", lambda v: v.tensor_mul(pm[:, 0:1], qm[:], km[:]),
         reads=["qm", "km"], writes=["pm"])
    p.op("s", lambda s: s.activation(cmax[:, 0:1], pm[:, 0:1], AF.Copy, scale=8.0),
         reads=["pm"], writes=["cmax"])
    p.op("s", lambda s: s.activation(cmax[:, 1:2], pm[:, 0:1], AF.Copy, scale=-8.0),
         reads=["pm", "cmax"], writes=["cmax"])
    p.op("t", lambda t: t.matmul(util_ps[:, 0:2], ones_bc[:], cmax[:],
         start=True, stop=True), reads=["onesb", "cmax"], writes=["util"])
    p.op("v", lambda v: v.tensor_copy(Cb[:], util_ps[:, 0:2]),
         reads=["util"], writes=["Cb"])

    for bb in range(B):
        for c in range(8):
            p.dma("gd", xxb[:, c * TH:(c + 1) * TH],
                  xxT[bb, c * 128:(c + 1) * 128, :], writes=[f"xxb{c}"])
        for st in range(5):
            for gi, grp in enumerate(("q", "k", "v")):
                for oc8 in range(8):
                    oc = gi * 8 + oc8
                    def mmg(t, st=st, oc=oc):
                        last = None
                        for kc in range(8):
                            off = 1 if kc < 4 else 0
                            col = kc * TH + off + st * 256
                            last = t.matmul(
                                proj_ps[:],
                                w_sb[:, kc * 3 * D + oc * 128: kc * 3 * D + (oc + 1) * 128],
                                xxb[:, col:col + 256],
                                start=(kc == 0), stop=(kc == 7))
                        return last
                    p.op("t", mmg, reads=[f"xxb{c}" for c in range(8)] + ["w_sb"],
                         writes=["proj"])
                    p.op("s", lambda s: s.activation(pre[:], proj_ps[:], AF.Copy),
                         reads=["proj"], writes=["pre"])
                    if grp == "v":
                        def trg(t):
                            last = None
                            for sub in range(2):
                                last = t.transpose(
                                    vt_ps[:, sub * 128:(sub + 1) * 128],
                                    pre[:, sub * 128:(sub + 1) * 128], idn[:])
                            return last
                        p.op("t", trg, reads=["pre", "idn"], writes=["vt_ps"])
                        for sub in range(2):
                            ktrel = st * 2 + sub
                            for par in range(2):
                                h = oc8 * 2 + par
                                p.op("v", lambda v, sub=sub, par=par, ktrel=ktrel, h=h:
                                     v.tensor_copy(
                                         vt_view[:, ktrel, h, 0:64],
                                         vt_ps[:, sub * 128 + par * 64:
                                               sub * 128 + par * 64 + 64]),
                                     reads=["vt_ps"], writes=[f"vt{ktrel}_{h}"])
                    else:
                        isq = grp == "q"
                        p.op("s", lambda s: s.activation(sq[:], pre[:], AF.Square),
                             reads=["pre"], writes=["sqt"])
                        p.op("t", lambda t: t.matmul(qn_ps[:], ind2[:], sq[:],
                             start=True, stop=True),
                             reads=["sqt", "ind2"], writes=["qn_ps"])
                        p.op("s", lambda s: s.activation(nrm[:], qn_ps[:], AF.Sqrt),
                             reads=["qn_ps"], writes=["nrm"])
                        p.op("v", lambda v: v.tensor_scalar(
                            rn_[:], nrm[:], 1e-12, None, ALU.max),
                            reads=["nrm"], writes=["rnA"])
                        p.op("v", lambda v: v.reciprocal(rn_[:], rn_[:]),
                             reads=["rnA"], writes=["rn"])
                        p.op("t", lambda t: t.matmul(util_ps[:], ind2Ts[:], rn_[:],
                             start=True, stop=True),
                             reads=["ind2T", "rn"], writes=["util"])
                        p.op("t", lambda t: t.matmul(wo_ps[:], p32s[:], pre[:],
                             start=True, stop=True), reads=["p32", "pre"],
                             writes=["wo_ps"])
                        scn = "qs128" if isq else "ks128"
                        scnw = "qs128sw" if isq else "ks128sw"
                        tC = "tabCq" if isq else "tabCk"
                        tS = "tabSq" if isq else "tabSk"
                        tcol = st * 256
                        p.op("v", lambda v, tcol=tcol, scn=scn, tC=tC: v.scalar_tensor_tensor(
                            t1[:], pre[:], qsc[scn][:, 0:1],
                            tab_sb[tC][:, tcol:tcol + 256], ALU.mult, ALU.mult),
                            reads=["pre", scn, tC], writes=["t1"])
                        p.op("v", lambda v, tcol=tcol, scnw=scnw, tS=tS: v.scalar_tensor_tensor(
                            t2[:], wo_ps[:], qsc[scnw][:, 0:1],
                            tab_sb[tS][:, tcol:tcol + 256], ALU.mult, ALU.mult),
                            reads=["wo_ps", scnw, tS], writes=["t2"])
                        p.op("v", lambda v: v.tensor_add(u1[:], t1[:], t2[:]),
                             reads=["t1", "t2"], writes=["u1"])
                        if isq:
                            if st > 0:
                                col = oc8 * TOWN + (st - 1) * 256
                                p.op("v", lambda v, col=col: v.tensor_mul(
                                    qTb[:, col:col + 256], u1[:], util_ps[:]),
                                    reads=["u1", "util"], writes=[f"q{oc8}"])
                        else:
                            col = oc8 * TQ + st * 256
                            p.op("v", lambda v, col=col: v.tensor_mul(
                                kTb[:, col:col + 256], u1[:], util_ps[:]),
                                reads=["u1", "util"], writes=[f"k{oc8}"])
        for w in range(4):
            mset = 1 if w == 0 else 0
            for h in range(H):
                oc8, par = h // 2, h % 2
                def qkg(t, w=w, h=h, oc8=oc8, par=par, bank=0):
                    last = None
                    for kc in range(2):
                        ktrel = 2 * w + kc
                        last = t.matmul(
                            score_ps[:, kc * 256:(kc + 1) * 256],
                            kTb[par * 64:(par + 1) * 64,
                                oc8 * TQ + ktrel * 128: oc8 * TQ + (ktrel + 1) * 128],
                            qTb[par * 64:(par + 1) * 64,
                                oc8 * TOWN + w * 256: oc8 * TOWN + (w + 1) * 256],
                            start=(kc == 0), stop=(kc == 1))
                    return last
                p.op("t", qkg, reads=[f"k{oc8}", f"q{oc8}"], writes=["scA"])
                def qkg2(t, w=w, h=h, oc8=oc8, par=par):
                    last = None
                    for kc in range(2, 4):
                        ktrel = 2 * w + kc
                        last = t.matmul(
                            score_ps[:, kc * 256:(kc + 1) * 256],
                            kTb[par * 64:(par + 1) * 64,
                                oc8 * TQ + ktrel * 128: oc8 * TQ + (ktrel + 1) * 128],
                            qTb[par * 64:(par + 1) * 64,
                                oc8 * TOWN + w * 256: oc8 * TOWN + (w + 1) * 256],
                            start=(kc == 2), stop=(kc == 3))
                    return last
                p.op("t", qkg2, reads=[f"k{oc8}", f"q{oc8}"], writes=["scB"])
                p.op("v", lambda v: v.tensor_scalar(
                    sc_f[:], score_ps[:], 8.0, Cb[:, 0:1], ALU.mult, ALU.min),
                    reads=["scA", "scB", "Cb"], writes=["sc_f"])
                p.op("s", lambda s: s.activation(
                    p_sb[:], sc_f[:], AF.Exp, bias=Cb[:, 1:2]),
                    reads=["sc_f", "Cb"], writes=["p_sb"])
                for kc in range(4):
                    p.op("v", lambda v, kc=kc, mset=mset: v.tensor_mul(
                        p_sb[:, kc * 256:(kc + 1) * 256],
                        p_sb[:, kc * 256:(kc + 1) * 256],
                        mask_sb[:, (mset * 4 + kc) * 256:(mset * 4 + kc + 1) * 256]),
                        reads=["p_sb", "mask"], writes=["p_sb"])
                for kc in range(4):
                    ktrel = 2 * w + kc
                    p.op("t", lambda t, ktrel=ktrel, kc=kc, h=h: t.matmul(
                        av_ps[:], vt_view[:, ktrel, h, :],
                        p_sb[:, kc * 256:(kc + 1) * 256],
                        start=(kc == 0), stop=(kc == 3)),
                        reads=[f"vt{ktrel}_{h}", "vtones", "p_sb"], writes=["av_ps"])
                p.op("v", lambda v: v.reciprocal(r1[:], av_ps[64:65, :]),
                     reads=["av_ps"], writes=["r1"])
                p.op("t", lambda t: t.matmul(util_ps[0:64, :], ones_bc[0:1, 0:64],
                     r1[:], start=True, stop=True),
                     reads=["onesb", "r1"], writes=["util"])
                p.op("v", lambda v: v.tensor_copy(rb_sb[:], util_ps[0:64, :]),
                     reads=["util"], writes=["rb_sb"])
                p.op("v", lambda v, oc8=oc8, par=par: v.tensor_mul(
                    o_asm[par * 64:(par + 1) * 64, oc8 * 256:(oc8 + 1) * 256],
                    av_ps[0:64, :], rb_sb[:]),
                    reads=["av_ps", "rb_sb"], writes=[f"oas{h}"])
            for oc in range(8):
                def wog(t, oc=oc):
                    last = None
                    for kc in range(8):
                        last = t.matmul(
                            wo_ps[:],
                            wo_sb[:, kc * D + oc * 128: kc * D + (oc + 1) * 128],
                            o_asm[:, kc * 256:(kc + 1) * 256],
                            start=(kc == 0), stop=(kc == 7))
                    return last
                p.op("t", wog, reads=[f"oas{h}" for h in range(H)] + ["wo_sb"],
                     writes=["wo_ps"])
                ri = oc % 2
                p.dma("sd", rxf[ri][:],
                      xxT[bb, oc * 128:(oc + 1) * 128, 257 + w * 256:257 + (w + 1) * 256],
                      writes=[f"rxf{ri}"])
                p.op("v", lambda v, ri=ri: v.tensor_add(
                    rxo[ri][:], rxf[ri][:], wo_ps[:]),
                    reads=[f"rxf{ri}", "wo_ps"], writes=[f"rxo{ri}"])
                p.dma("sd", xxoT[bb, oc * 128:(oc + 1) * 128, w * 256:(w + 1) * 256],
                      rxo[ri][:], reads=[f"rxo{ri}"])

    p.fence_all_stores("v", lambda v: v.tensor_copy(r1[0:1, 0:1], r1[0:1, 0:1]))
    p.emit()
    ctx.close()
    return nc



def build_Bffn():
    nc = bass.Bass("TRN2", target_bir_lowering=False, debug=False)
    TH = TOWN + 1
    M1 = 5632
    K2 = 2816
    xxT = nc.dram_tensor("xxT", [B, D, TH], F32, kind="ExternalInput").ap()
    w1d = nc.dram_tensor("w1p", [D, M1], F32, kind="ExternalInput").ap()
    w2d = nc.dram_tensor("w2p", [K2, D], F32, kind="ExternalInput").ap()
    b1d = nc.dram_tensor("b1p", [128, 44], F32, kind="ExternalInput").ap()
    b2d = nc.dram_tensor("b2c", [128, 8], F32, kind="ExternalInput").ap()
    gd_ = nc.dram_tensor("gc", [128, 8], F32, kind="ExternalInput").ap()
    blnd = nc.dram_tensor("blnc", [128, 8], F32, kind="ExternalInput").ap()
    xxoT = nc.dram_tensor("xxoT", [B, D, TOWN], F32, kind="ExternalOutput").ap()

    ctx = ExitStack()
    sb = lambda n, s, d=F32: ctx.enter_context(nc.sbuf_tensor(n, s, d))
    pt = lambda n, s, d=F32: ctx.enter_context(nc.psum_tensor(n, s, d))

    w1 = sb("w1", [128, 8 * M1], BF16)
    w2 = sb("w2", [128, 22 * D], BF16)
    xxb = sb("xxb", [128, 8 * TH], BF16)
    b1 = sb("b1", [128, 44])
    b2 = sb("b2", [128, 8])
    gc = sb("gcs", [128, 8])
    bln = sb("blns", [128, 8])
    xn = sb("xn", [128, 512], BF16)
    xng = [sb(f"xng{i}", [128, 512], BF16) for i in range(8)]
    ga = sb("ga", [128, 512], BF16)
    gb = sb("gb", [128, 512], BF16)
    uu = sb("uu", [128, 22 * 512], BF16)
    tmp2 = sb("tmp2", [128, 512], F32)
    rxf = [sb(f"rxf{i}", [128, 512], F32) for i in range(2)]
    rxo = [sb(f"rxo{i}", [128, 512], F32) for i in range(2)]

    psA = pt("psA", [128, 512])
    psB = pt("psB", [128, 512])
    w2ps = pt("w2ps", [128, 512])

    p = Prog(nc)
    r = alloc_ln_scratch(nc, ctx, p)

    p.dma("gd", w1[:].rearrange("p (k m) -> p k m", k=8),
          w1d.rearrange("(k p) m -> p k m", p=128), writes=["w1"])
    p.dma("gd", w2[:].rearrange("p (k m) -> p k m", k=22),
          w2d.rearrange("(k p) m -> p k m", p=128), writes=["w2"])
    p.dma("sd", b1[:], b1d, writes=["b1"])
    p.dma("sd", b2[:], b2d, writes=["b2"])
    p.dma("sd", gc[:], gd_, writes=["gc"])
    p.dma("sd", bln[:], blnd, writes=["bln"])

    def shcol(bb, c, nt):
        off = 1 if c < 4 else 0
        return c * TH + off + nt * 512

    for bb in range(B):
        for c in range(8):
            p.dma("gd", xxb[:, c * TH:(c + 1) * TH],
                  xxT[bb, c * 128:(c + 1) * 128, :], writes=[f"xxbuf{c}"])
        for nt in range(TOWN // 512):
            xkeys = [f"xxbuf{c}" for c in range(8)]
            for c in range(8):
                col = shcol(bb, c, nt)
                p.op("s", lambda s, col=col: s.activation(
                    r["sq"][:], xxb[:, col:col + 512], AF.Square),
                    reads=[f"xxbuf{c}"], writes=["sqt"])
                p.op("t", lambda t, col=col, c=c: t.matmul(
                    r["stats_x"][:], r["ones_red"][:], xxb[:, col:col + 512],
                    start=(c == 0), stop=(c == 7)),
                    reads=[f"xxbuf{c}", "ones"], writes=["stats_x"])
                p.op("t", lambda t, c=c: t.matmul(
                    r["stats_xq"][:], r["ones_red"][:], r["sq"][:],
                    start=(c == 0), stop=(c == 7)),
                    reads=["sqt", "ones"], writes=["stats_xq"])
            p.op("s", lambda s: s.activation(
                r["mu"][:], r["stats_x"][:], AF.Copy, scale=1.0 / D),
                reads=["stats_x"], writes=["mu"])
            p.op("s", lambda s: s.activation(r["musq"][:], r["mu"][:], AF.Square),
                 reads=["mu"], writes=["musq"])
            p.op("v", lambda v: v.scalar_tensor_tensor(
                r["varr"][:], r["stats_xq"][:], 1.0 / D, r["musq"][:],
                ALU.mult, ALU.subtract),
                reads=["stats_xq", "musq"], writes=["varr"])
            p.op("s", lambda s: s.activation(
                r["sd_"][:], r["varr"][:], AF.Sqrt, bias=r["epsc"][0:1, 0:1]),
                reads=["varr", "epsc"], writes=["sd"])
            p.op("v", lambda v: v.reciprocal(r["rstd"][:], r["sd_"][:]),
                 reads=["sd"], writes=["rstd"])
            p.op("v", lambda v: v.scalar_tensor_tensor(
                r["nmr"][:], r["mu"][:], -1.0, r["rstd"][:], ALU.mult, ALU.mult),
                reads=["mu", "rstd"], writes=["nmr"])
            p.op("t", lambda t: t.matmul(
                r["Aps"][:], r["ones_bc"][:], r["rstd"][:], start=True, stop=True),
                reads=["rstd", "onesb"], writes=["Aps"])
            p.op("t", lambda t: t.matmul(
                r["Bps"][:], r["ones_bc"][:], r["nmr"][:], start=True, stop=True),
                reads=["nmr", "onesb"], writes=["Bps"])
            for c in range(8):
                col = shcol(bb, c, nt)
                p.op("v", lambda v, col=col: v.tensor_mul(
                    r["tmpf"][:], xxb[:, col:col + 512], r["Aps"][:]),
                    reads=[f"xxbuf{c}", "Aps"], writes=["tmpf"])
                p.op("v", lambda v: v.tensor_add(tmp2[:], r["tmpf"][:], r["Bps"][:]),
                     reads=["tmpf", "Bps"], writes=["tmp2"])
                p.op("v", lambda v, c=c: v.tensor_scalar(
                    xng[c][:], tmp2[:], gc[:, c:c + 1], bln[:, c:c + 1],
                    ALU.mult, ALU.add),
                    reads=["tmp2", "gc", "bln"], writes=[f"xn{c}"])
            w2keys = []
            for pair in range(22):
                ui = pair % 2
                def mma(t, pair=pair):
                    last = None
                    for kc in range(8):
                        last = t.matmul(
                            psA[:], w1[:, kc * M1 + pair * 128: kc * M1 + (pair + 1) * 128],
                            xng[kc][:], start=(kc == 0), stop=(kc == 7))
                    return last
                def mmg_(t, pair=pair):
                    last = None
                    oc = 22 + pair
                    for kc in range(8):
                        last = t.matmul(
                            psB[:], w1[:, kc * M1 + oc * 128: kc * M1 + (oc + 1) * 128],
                            xng[kc][:], start=(kc == 0), stop=(kc == 7))
                    return last
                p.op("t", mma, reads=[f"xn{c}" for c in range(8)] + ["w1"], writes=["psA"])
                p.op("t", mmg_, reads=[f"xn{c}" for c in range(8)] + ["w1"], writes=["psB"])
                p.op("s", lambda s, pair=pair: s.activation(
                    ga[:], psA[:], AF.Identity, bias=b1[:, pair:pair + 1]),
                    reads=["psA", "b1"], writes=["ga"])
                p.op("s", lambda s, pair=pair: s.activation(
                    gb[:], psB[:], AF.Gelu, bias=b1[:, 22 + pair:22 + pair + 1]),
                    reads=["psB", "b1"], writes=["gb"])
                p.op("v", lambda v, pair=pair: v.tensor_mul(
                    uu[:, pair * 512:(pair + 1) * 512], ga[:], gb[:]),
                    reads=["ga", "gb"], writes=[f"uu{pair}"])
            for oc in range(8):
                def w2g(t, oc=oc):
                    last = None
                    for pair in range(22):
                        last = t.matmul(
                            w2ps[:], w2[:, pair * D + oc * 128: pair * D + (oc + 1) * 128],
                            uu[:, pair * 512:(pair + 1) * 512],
                            start=(pair == 0), stop=(pair == 21))
                    return last
                p.op("t", w2g, reads=[f"uu{i}" for i in range(22)] + ["w2"],
                     writes=["w2ps"])
                ri = oc % 2
                p.dma("sd", rxf[ri][:],
                      xxT[bb, oc * 128:(oc + 1) * 128, 1 + nt * 512:1 + (nt + 1) * 512],
                      writes=[f"rxf{ri}"])
                p.op("v", lambda v, ri=ri, oc=oc: v.scalar_tensor_tensor(
                    rxo[ri][:], w2ps[:], b2[:, oc:oc + 1], rxf[ri][:],
                    ALU.add, ALU.add),
                    reads=["w2ps", "b2", f"rxf{ri}"], writes=[f"rxo{ri}"])
                p.dma("sd", xxoT[bb, oc * 128:(oc + 1) * 128, nt * 512:(nt + 1) * 512],
                      rxo[ri][:], reads=[f"rxo{ri}"])

    p.fence_all_stores("v", lambda v: v.tensor_copy(tmp2[:, 0:1], tmp2[:, 0:1]))
    p.emit()
    ctx.close()
    return nc



def build_BC(debug=False):
    nc = bass.Bass("TRN2", target_bir_lowering=False, debug=False)
    T = TOWN
    NT = T // 512
    x1d = nc.dram_tensor("x1T", [B, D, T], F32, kind="ExternalInput").ap()
    xxd = nc.dram_tensor("xxT", [B, D, T], F32, kind="ExternalInput").ap()
    wind = nc.dram_tensor("winp", [D, 3 * D], F32, kind="ExternalInput").ap()
    woutd = nc.dram_tensor("woutp", [D, D], F32, kind="ExternalInput").ap()
    lbd = nc.dram_tensor("lbc", [128, 8], F32, kind="ExternalInput").ap()
    pErd = nc.dram_tensor("pairE", [128, 128], F32, kind="ExternalInput").ap()
    x3d = nc.dram_tensor("x3T", [B, D, T], F32, kind="ExternalOutput").ap()
    if debug:
        zdbg = nc.dram_tensor("zdbg", [B, D, T], BF16, kind="ExternalOutput").ap()
        odbg = nc.dram_tensor("odbg", [B, D, T], BF16, kind="ExternalOutput").ap()

    ctx = ExitStack()
    sb = lambda n, s, d=F32: ctx.enter_context(nc.sbuf_tensor(n, s, d))
    pt = lambda n, s, d=F32: ctx.enter_context(nc.psum_tensor(n, s, d))

    win = sb("win", [128, 8 * 3 * D], BF16)
    wout = sb("wout", [128, 8 * D], BF16)
    x2b = sb("x2b", [128, B * 8 * T], BF16)
    znC = sb("znC", [128, B * 8 * T], BF16)
    ft = {}
    for gname in ("V", "Q", "F"):
        for bb in range(B):
            ft[(gname, bb)] = sb(f"ft{gname}{bb}", [128, 512], F32)
    lb_sb = sb("lb_sb", [128, 8])
    omlb = sb("omlb", [128, 8])
    pE = sb("pE", [128, 128], F32)
    xn = sb("xnC", [128, 512], BF16)
    m1 = sb("m1", [128, 512])
    m2 = sb("m2", [128, 512])
    ot = sb("ot", [128, 512])
    oT = sb("oTC", [128, B * 8 * T], BF16)
    rx1 = [sb(f"rx1{i}", [128, 512], F32) for i in range(2)]
    rx2 = [sb(f"rx2{i}", [128, 512], F32) for i in range(2)]
    rxo = [sb(f"rxoC{i}", [128, 512], F32) for i in range(2)]

    proj = pt("projC", [128, 512])
    dex = pt("dex", [128, 512])
    wops = pt("wops", [128, 512])

    p = Prog(nc)
    r = alloc_ln_scratch(nc, ctx, p)

    p.dma("gd", win[:].rearrange("p (k m) -> p k m", k=8),
          wind.rearrange("(k p) m -> p k m", p=128), writes=["win"])
    p.dma("gd", wout[:].rearrange("p (k m) -> p k m", k=8),
          woutd.rearrange("(k p) m -> p k m", p=128), writes=["wout"])
    p.dma("sd", lb_sb[:], lbd, writes=["lb"])
    p.dma("sd", pE[:], pErd, writes=["pE"])
    p.op("v", lambda v: v.tensor_scalar(omlb[:], lb_sb[:], -1.0, 1.0,
         ALU.mult, ALU.add), reads=["lb"], writes=["omlb"])

    for bb in range(B):
        for c in range(8):
            for nt in range(NT):
                ri = (c * NT + nt) % 2
                col = (bb * 8 + c) * T + nt * 512
                p.dma("sd", rx1[ri][:],
                      x1d[bb, c * 128:(c + 1) * 128, nt * 512:(nt + 1) * 512],
                      writes=[f"rx1{ri}"])
                p.dma("sd", rx2[ri][:],
                      xxd[bb, c * 128:(c + 1) * 128, nt * 512:(nt + 1) * 512],
                      writes=[f"rx2{ri}"])
                p.op("v", lambda v, ri=ri, col=col: v.tensor_add(
                    x2b[:, col:col + 512], rx1[ri][:], rx2[ri][:]),
                    reads=[f"rx1{ri}", f"rx2{ri}"], writes=[f"x2{bb}_{c}_{nt}"])

    for bb in range(B):
        for nt in range(NT):
            for c in range(8):
                col = (bb * 8 + c) * T + nt * 512
                p.op("s", lambda s, col=col: s.activation(
                    r["sq"][:], x2b[:, col:col + 512], AF.Square),
                    reads=[f"x2{bb}_{c}_{nt}"], writes=["sqt"])
                p.op("t", lambda t, col=col, c=c: t.matmul(
                    r["stats_x"][:], r["ones_red"][:], x2b[:, col:col + 512],
                    start=(c == 0), stop=(c == 7)),
                    reads=[f"x2{bb}_{c}_{nt}", "ones"], writes=["stats_x"])
                p.op("t", lambda t, c=c: t.matmul(
                    r["stats_xq"][:], r["ones_red"][:], r["sq"][:],
                    start=(c == 0), stop=(c == 7)),
                    reads=["sqt", "ones"], writes=["stats_xq"])
            p.op("s", lambda s: s.activation(
                r["mu"][:], r["stats_x"][:], AF.Copy, scale=1.0 / D),
                reads=["stats_x"], writes=["mu"])
            p.op("s", lambda s: s.activation(r["musq"][:], r["mu"][:], AF.Square),
                 reads=["mu"], writes=["musq"])
            p.op("v", lambda v: v.scalar_tensor_tensor(
                r["varr"][:], r["stats_xq"][:], 1.0 / D, r["musq"][:],
                ALU.mult, ALU.subtract), reads=["stats_xq", "musq"], writes=["varr"])
            p.op("s", lambda s: s.activation(
                r["sd_"][:], r["varr"][:], AF.Sqrt, bias=r["epsc"][0:1, 0:1]),
                reads=["varr", "epsc"], writes=["sd"])
            p.op("v", lambda v: v.reciprocal(r["rstd"][:], r["sd_"][:]),
                 reads=["sd"], writes=["rstd"])
            p.op("v", lambda v: v.scalar_tensor_tensor(
                r["nmr"][:], r["mu"][:], -1.0, r["rstd"][:], ALU.mult, ALU.mult),
                reads=["mu", "rstd"], writes=["nmr"])
            p.op("t", lambda t: t.matmul(
                r["Aps"][:], r["ones_bc"][:], r["rstd"][:], start=True, stop=True),
                reads=["rstd", "onesb"], writes=["Aps"])
            p.op("t", lambda t: t.matmul(
                r["Bps"][:], r["ones_bc"][:], r["nmr"][:], start=True, stop=True),
                reads=["nmr", "onesb"], writes=["Bps"])
            for c in range(8):
                col = (bb * 8 + c) * T + nt * 512
                p.op("v", lambda v, col=col: v.tensor_mul(
                    r["tmpf"][:], x2b[:, col:col + 512], r["Aps"][:]),
                    reads=[f"x2{bb}_{c}_{nt}", "Aps"], writes=["tmpf"])
                p.op("v", lambda v, col=col: v.tensor_add(
                    znC[:, col:col + 512], r["tmpf"][:], r["Bps"][:]),
                    reads=["tmpf", "Bps"], writes=[f"znC{bb}_{c}_{nt}"])

    gmap = {"V": 0, "Q": 1, "F": 2}
    for c in range(8):
        for nt in range(NT):
            for gname in ("V", "Q", "F"):
                for bb in range(B):
                    oc = gmap[gname] * 8 + c
                    def mmg(t, bb=bb, nt=nt, oc=oc):
                        last = None
                        for kc in range(8):
                            col = (bb * 8 + kc) * T + nt * 512
                            last = t.matmul(
                                proj[:],
                                win[:, kc * 3 * D + oc * 128: kc * 3 * D + (oc + 1) * 128],
                                znC[:, col:col + 512], start=(kc == 0), stop=(kc == 7))
                        return last
                    p.op("t", mmg, reads=[f"znC{bb}_{kc}_{nt}" for kc in range(8)] +
                         ["win"], writes=["projC"])
                    func = AF.Sigmoid if gname == "F" else AF.Gelu
                    p.op("s", lambda s, gname=gname, bb=bb, func=func: s.activation(
                        ft[(gname, bb)][:], proj[:], func),
                        reads=["projC"], writes=[f"ft{gname}{bb}"])
            sl = lambda g, bb: ft[(g, bb)][:]
            rd = lambda g, bb: f"ft{g}{bb}"
            for bb in range(B):
                p.op("v", lambda v, bb=bb, c=c: v.tensor_scalar(
                    sl("F", bb), sl("F", bb), omlb[:, c:c + 1], lb_sb[:, c:c + 1],
                    ALU.mult, ALU.add),
                    reads=[rd("F", bb), "omlb", "lb"], writes=[rd("F", bb)])
            p.op("v", lambda v: v.tensor_scalar(
                m1[:], sl("F", 0), -1.0, 1.0, ALU.mult, ALU.add),
                reads=[rd("F", 0)], writes=["m1"])
            p.op("v", lambda v: v.tensor_mul(m2[:], sl("Q", 0), m1[:]),
                 reads=[rd("Q", 0), "m1"], writes=["m2"])
            p.op("t", lambda t: t.matmul(dex[:], pE[:], m2[:], start=True, stop=True),
                 reads=["pE", "m2"], writes=["dex"])
            col = (0 * 8 + c) * T + nt * 512
            p.op("v", lambda v, col=col: v.tensor_mul(
                oT[:, col:col + 512], dex[:], sl("V", 0)),
                reads=["dex", rd("V", 0)], writes=[f"o0_{c}_{nt}"])
            p.op("v", lambda v: v.tensor_mul(m2[:], sl("F", 1), m1[:]),
                 reads=[rd("F", 1), "m1"], writes=["m2"])
            p.op("v", lambda v: v.tensor_mul(m2[:], sl("Q", 1), m2[:]),
                 reads=[rd("Q", 1), "m2"], writes=["m2"])
            p.op("t", lambda t: t.matmul(dex[:], pE[:], m2[:], start=True, stop=True),
                 reads=["pE", "m2"], writes=["dex"])
            p.op("v", lambda v: v.tensor_mul(ot[:], dex[:], sl("V", 0)),
                 reads=["dex", rd("V", 0)], writes=["otC"])
            p.op("v", lambda v: v.tensor_scalar(
                m1[:], sl("F", 1), -1.0, 1.0, ALU.mult, ALU.add),
                reads=[rd("F", 1)], writes=["m1"])
            p.op("v", lambda v: v.tensor_mul(m2[:], sl("Q", 1), m1[:]),
                 reads=[rd("Q", 1), "m1"], writes=["m2"])
            p.op("t", lambda t: t.matmul(dex[:], pE[:], m2[:], start=True, stop=True),
                 reads=["pE", "m2"], writes=["dex"])
            p.op("v", lambda v: v.tensor_mul(m2[:], dex[:], sl("V", 1)),
                 reads=["dex", rd("V", 1)], writes=["m2"])
            col = (1 * 8 + c) * T + nt * 512
            p.op("v", lambda v, col=col: v.tensor_add(
                oT[:, col:col + 512], ot[:], m2[:]),
                reads=["otC", "m2"], writes=[f"o1_{c}_{nt}"])

    for bb in range(B):
        for oc in range(8):
            for nt in range(NT):
                def mmo(t, bb=bb, oc=oc, nt=nt):
                    last = None
                    for kc in range(8):
                        col = (bb * 8 + kc) * T + nt * 512
                        last = t.matmul(
                            wops[:], wout[:, kc * D + oc * 128: kc * D + (oc + 1) * 128],
                            oT[:, col:col + 512], start=(kc == 0), stop=(kc == 7))
                    return last
                p.op("t", mmo, reads=[f"o{bb}_{kc}_{nt}" for kc in range(8)] +
                     ["wout"], writes=["wops"])
                ri = (oc * NT + nt) % 2
                p.dma("sd", rx1[ri][:],
                      x1d[bb, oc * 128:(oc + 1) * 128, nt * 512:(nt + 1) * 512],
                      writes=[f"rx1{ri}"])
                p.dma("sd", rx2[ri][:],
                      xxd[bb, oc * 128:(oc + 1) * 128, nt * 512:(nt + 1) * 512],
                      writes=[f"rx2{ri}"])
                p.op("v", lambda v, ri=ri: v.tensor_add(
                    rxo[ri][:], rx1[ri][:], rx2[ri][:]),
                    reads=[f"rx1{ri}", f"rx2{ri}"], writes=[f"rxoC{ri}"])
                p.op("v", lambda v, ri=ri: v.tensor_add(
                    rxo[ri][:], rxo[ri][:], wops[:]),
                    reads=[f"rxoC{ri}", "wops"], writes=[f"rxoC{ri}"])
                p.dma("sd", x3d[bb, oc * 128:(oc + 1) * 128, nt * 512:(nt + 1) * 512],
                      rxo[ri][:], reads=[f"rxoC{ri}"])

    if debug:
        for bb in range(B):
            for c in range(8):
                p.dma("sd", zdbg[bb, c * 128:(c + 1) * 128, :],
                      znC[:, (bb * 8 + c) * T:(bb * 8 + c + 1) * T],
                      reads=[f"znC{bb}_{c}_{nt}" for nt in range(NT)])
                p.dma("sd", odbg[bb, c * 128:(c + 1) * 128, :],
                      oT[:, (bb * 8 + c) * T:(bb * 8 + c + 1) * T],
                      reads=[f"o{bb}_{c}_{nt}" for nt in range(NT)])
    p.fence_all_stores("v", lambda v: v.tensor_copy(m1[0:1, 0:1], m1[0:1, 0:1]))
    p.emit()
    ctx.close()
    return nc



def build_D():
    nc = bass.Bass("TRN2", target_bir_lowering=False, debug=False)
    T = TOWN
    NT = T // 512
    x3d = nc.dram_tensor("x3T", [B, D, T], F32, kind="ExternalInput").ap()
    wd = {nm: nc.dram_tensor(nm, [D, D], F32, kind="ExternalInput").ap()
          for nm in ("wfT", "ufT", "wiT", "uiT", "wcT", "ucT", "woT2", "uoT")}
    bd = {nm: nc.dram_tensor(nm, [128, 8], F32, kind="ExternalInput").ap()
          for nm in ("bfc", "bic", "bcc", "boc", "incc")}
    x4d = nc.dram_tensor("x4T", [B, D, T], F32, kind="ExternalOutput").ap()

    ctx = ExitStack()
    sb = lambda n, s, d=F32: ctx.enter_context(nc.sbuf_tensor(n, s, d))
    pt = lambda n, s, d=F32: ctx.enter_context(nc.psum_tensor(n, s, d))

    x3b = sb("x3b", [128, B * 8 * T], BF16)
    zn = sb("znD", [128, B * 8 * T], BF16)
    cs = sb("cs", [128, 8 * T], F32)
    cb = sb("cb", [128, 8 * T], BF16)
    f_sb = sb("f_sb", [128, 8 * T], BF16)
    i_sb = sb("i_sb", [128, 8 * T], BF16)
    g_t = sb("g_t", [128, 512], BF16)
    wbuf = sb("wbuf", [128, 8 * D], BF16)
    ubuf = sb("ubuf", [128, 8 * D], BF16)
    t1D = sb("t1D", [128, 512], F32)
    u1D = sb("u1D", [128, 512], F32)
    hf = sb("hf", [128, 512], F32)
    rxf = [sb(f"rxfD{i}", [128, 512], F32) for i in range(2)]
    rxo = [sb(f"rxoD{i}", [128, 512], F32) for i in range(2)]
    bsb = {nm: sb(nm + "s", [128, 8]) for nm in bd}
    gate_ps = pt("gate_ps", [128, 512])

    p = Prog(nc)
    r = alloc_ln_scratch(nc, ctx, p)

    for bb in range(B):
        for c in range(8):
            p.dma("gd", x3b[:, (bb * 8 + c) * T:(bb * 8 + c + 1) * T],
                  x3d[bb, c * 128:(c + 1) * 128, :], writes=[f"x3{bb}_{c}"])
    for nm in bd:
        p.dma("sd", bsb[nm][:], bd[nm], writes=[nm])

    for bb in range(B):
        for nt in range(NT):
            for c in range(8):
                col = (bb * 8 + c) * T + nt * 512
                p.op("s", lambda s, col=col: s.activation(
                    r["sq"][:], x3b[:, col:col + 512], AF.Square),
                    reads=[f"x3{bb}_{c}"], writes=["sqt"])
                p.op("t", lambda t, col=col, c=c: t.matmul(
                    r["stats_x"][:], r["ones_red"][:], x3b[:, col:col + 512],
                    start=(c == 0), stop=(c == 7)),
                    reads=[f"x3{bb}_{c}", "ones"], writes=["stats_x"])
                p.op("t", lambda t, c=c: t.matmul(
                    r["stats_xq"][:], r["ones_red"][:], r["sq"][:],
                    start=(c == 0), stop=(c == 7)),
                    reads=["sqt", "ones"], writes=["stats_xq"])
            p.op("s", lambda s: s.activation(
                r["mu"][:], r["stats_x"][:], AF.Copy, scale=1.0 / D),
                reads=["stats_x"], writes=["mu"])
            p.op("s", lambda s: s.activation(r["musq"][:], r["mu"][:], AF.Square),
                 reads=["mu"], writes=["musq"])
            p.op("v", lambda v: v.scalar_tensor_tensor(
                r["varr"][:], r["stats_xq"][:], 1.0 / D, r["musq"][:],
                ALU.mult, ALU.subtract), reads=["stats_xq", "musq"], writes=["varr"])
            p.op("s", lambda s: s.activation(
                r["sd_"][:], r["varr"][:], AF.Sqrt, bias=r["epsc"][0:1, 0:1]),
                reads=["varr", "epsc"], writes=["sd"])
            p.op("v", lambda v: v.reciprocal(r["rstd"][:], r["sd_"][:]),
                 reads=["sd"], writes=["rstd"])
            p.op("v", lambda v: v.scalar_tensor_tensor(
                r["nmr"][:], r["mu"][:], -1.0, r["rstd"][:], ALU.mult, ALU.mult),
                reads=["mu", "rstd"], writes=["nmr"])
            p.op("t", lambda t: t.matmul(
                r["Aps"][:], r["ones_bc"][:], r["rstd"][:], start=True, stop=True),
                reads=["rstd", "onesb"], writes=["Aps"])
            p.op("t", lambda t: t.matmul(
                r["Bps"][:], r["ones_bc"][:], r["nmr"][:], start=True, stop=True),
                reads=["nmr", "onesb"], writes=["Bps"])
            for c in range(8):
                col = (bb * 8 + c) * T + nt * 512
                p.op("v", lambda v, col=col: v.tensor_mul(
                    r["tmpf"][:], x3b[:, col:col + 512], r["Aps"][:]),
                    reads=[f"x3{bb}_{c}", "Aps"], writes=["tmpf"])
                p.op("v", lambda v, col=col: v.tensor_add(
                    zn[:, col:col + 512], r["tmpf"][:], r["Bps"][:]),
                    reads=["tmpf", "Bps"], writes=[f"zn{bb}_{c}"])

    for c in range(8):
        for nt in range(NT):
            col = c * T + nt * 512
            p.op("s", lambda s, col=col, c=c: s.activation(
                cs[:, col:col + 512], zn[:, 0:512], AF.Identity,
                scale=0.0, bias=bsb["incc"][:, c:c + 1]),
                reads=["incc"], writes=[f"cs{c}"])

    def gate(step, wn, un, bn, func, dstname, dst):
        p.dma("gd", wbuf[:].rearrange("p (k m) -> p k m", k=8),
              wd[wn].rearrange("(k p) m -> p k m", p=128), writes=["wbuf"])
        p.dma("gd", ubuf[:].rearrange("p (k m) -> p k m", k=8),
              wd[un].rearrange("(k p) m -> p k m", p=128), writes=["ubuf"])
        for oc in range(8):
            for nt in range(NT):
                def mmg(t, oc=oc, nt=nt, step=step):
                    last = None
                    for kc in range(8):
                        zcol = (step * 8 + kc) * T + nt * 512
                        last = t.matmul(
                            gate_ps[:], wbuf[:, kc * D + oc * 128: kc * D + (oc + 1) * 128],
                            zn[:, zcol:zcol + 512], start=(kc == 0), stop=False)
                    for kc in range(8):
                        ccol = kc * T + nt * 512
                        last = t.matmul(
                            gate_ps[:], ubuf[:, kc * D + oc * 128: kc * D + (oc + 1) * 128],
                            cb[:, ccol:ccol + 512], start=False, stop=(kc == 7))
                    return last
                p.op("t", mmg,
                     reads=[f"zn{step}_{kc}" for kc in range(8)] +
                           [f"cb{kc}" for kc in range(8)] + ["wbuf", "ubuf"],
                     writes=["gate_ps"])
                col = oc * T + nt * 512
                if dst is not None:
                    p.op("s", lambda s, col=col, oc=oc: s.activation(
                        dst[:, col:col + 512], gate_ps[:], func,
                        bias=bsb[bn][:, oc:oc + 1]),
                        reads=["gate_ps", bn], writes=[f"{dstname}{oc}"])
                else:
                    p.op("s", lambda s, col=col, oc=oc: s.activation(
                        g_t[:], gate_ps[:], func, bias=bsb[bn][:, oc:oc + 1]),
                        reads=["gate_ps", bn], writes=["g_t"])
                    p.op("v", lambda v, col=col: v.tensor_mul(
                        t1D[:], f_sb[:, col:col + 512], cs[:, col:col + 512]),
                        reads=[f"fsb{oc}", f"cs{oc}"], writes=["t1D"])
                    p.op("v", lambda v, col=col: v.tensor_mul(
                        u1D[:], i_sb[:, col:col + 512], g_t[:]),
                        reads=[f"isb{oc}", "g_t"], writes=["u1D"])
                    p.op("v", lambda v, col=col: v.tensor_add(
                        cs[:, col:col + 512], t1D[:], u1D[:]),
                        reads=["t1D", "u1D"], writes=[f"cs{oc}"])

    for step in range(B):
        for c in range(8):
            for nt in range(NT):
                col = c * T + nt * 512
                p.op("s", lambda s, col=col, c=c: s.activation(
                    cb[:, col:col + 512], cs[:, col:col + 512], AF.Copy),
                    reads=[f"cs{c}"], writes=[f"cb{c}"])
        gate(step, "wfT", "ufT", "bfc", AF.Sigmoid, "fsb", f_sb)
        gate(step, "wiT", "uiT", "bic", AF.Sigmoid, "isb", i_sb)
        gate(step, "wcT", "ucT", "bcc", AF.Tanh, "g", None)

    p.dma("gd", wbuf[:].rearrange("p (k m) -> p k m", k=8),
          wd["woT2"].rearrange("(k p) m -> p k m", p=128), writes=["wbuf"])
    p.dma("gd", ubuf[:].rearrange("p (k m) -> p k m", k=8),
          wd["uoT"].rearrange("(k p) m -> p k m", p=128), writes=["ubuf"])
    for c in range(8):
        for nt in range(NT):
            col = c * T + nt * 512
            p.op("s", lambda s, col=col, c=c: s.activation(
                cb[:, col:col + 512], cs[:, col:col + 512], AF.Copy),
                reads=[f"cs{c}"], writes=[f"cb{c}"])
    for step in range(B):
        for oc in range(8):
            for nt in range(NT):
                def mmh(t, oc=oc, nt=nt, step=step):
                    last = None
                    for kc in range(8):
                        zcol = (step * 8 + kc) * T + nt * 512
                        last = t.matmul(
                            gate_ps[:], wbuf[:, kc * D + oc * 128: kc * D + (oc + 1) * 128],
                            zn[:, zcol:zcol + 512], start=(kc == 0), stop=False)
                    for kc in range(8):
                        ccol = kc * T + nt * 512
                        last = t.matmul(
                            gate_ps[:], ubuf[:, kc * D + oc * 128: kc * D + (oc + 1) * 128],
                            cb[:, ccol:ccol + 512], start=False, stop=(kc == 7))
                    return last
                p.op("t", mmh,
                     reads=[f"zn{step}_{kc}" for kc in range(8)] +
                           [f"cb{kc}" for kc in range(8)] + ["wbuf", "ubuf"],
                     writes=["gate_ps"])
                p.op("s", lambda s, oc=oc: s.activation(
                    hf[:], gate_ps[:], AF.Tanh, bias=bsb["boc"][:, oc:oc + 1]),
                    reads=["gate_ps", "boc"], writes=["hf"])
                ri = (oc * NT + nt) % 2
                p.dma("sd", rxf[ri][:],
                      x3d[step, oc * 128:(oc + 1) * 128, nt * 512:(nt + 1) * 512],
                      writes=[f"rxfD{ri}"])
                p.op("v", lambda v, ri=ri: v.tensor_add(
                    rxo[ri][:], rxf[ri][:], hf[:]),
                    reads=[f"rxfD{ri}", "hf"], writes=[f"rxoD{ri}"])
                p.dma("sd", x4d[step, oc * 128:(oc + 1) * 128, nt * 512:(nt + 1) * 512],
                      rxo[ri][:], reads=[f"rxoD{ri}"])

    p.fence_all_stores("v", lambda v: v.tensor_copy(hf[0:1, 0:1], hf[0:1, 0:1]))
    p.emit()
    ctx.close()
    return nc



def host_tables_B(t0):
    n = S
    d = np.arange(128) % 64
    half = np.arange(0, HD, 2, dtype=np.float64)
    inv_freq = 1.0 / 10000.0 ** (half / HD)
    fr = inv_freq[d % 32]
    base = (half + 0.4 * HD) / (1.4 * HD)
    bs = base[d % 32]
    t = np.arange(t0 - 256, t0 + TOWN, dtype=np.float64)
    ang = t[None, :] * fr[:, None]
    xs = bs[:, None] ** ((t[None, :] - n // 2) / 128.0)
    sgn = np.where(d[:, None] % 64 < 32, -1.0, 1.0)
    return (np.cos(ang) * xs, sgn * np.sin(ang) * xs,
            np.cos(ang) / xs, sgn * np.sin(ang) / xs)

def host_p32():
    P = np.zeros((128, 128), np.float32)
    for i in range(128):
        blk, dd = i // 64 * 64, i % 64
        P[blk + (dd + 32) % 64, i] = 1.0
    return P

def host_masks_std():
    kc = np.arange(4)[:, None, None]
    s_ = np.arange(128)[None, :, None]
    rr = np.arange(256)[None, None, :]
    u = kc * 128 + s_
    return ((rr <= u) & (u <= rr + 256)).astype(np.float32)

def host_pairE():
    pp = np.arange(128)[:, None]
    rr = np.arange(128)[None, :]
    return (pp // 2 == rr // 2).astype(np.float32)

def host_prep_Battn_shared(I, layer):
    Wqkv = np.ascontiguousarray(I["L0_ffn_Wqkv"][layer].astype(np.float32))
    Wo = np.ascontiguousarray(I["L0_ffn_Wo"][layer].astype(np.float32))
    qs = I["L0_ffn_q_scale"][layer].astype(np.float32)
    ks = I["L0_ffn_k_scale"][layer].astype(np.float32)
    mstd = host_masks_std()
    mw0 = mstd.copy(); mw0[0] = 0; mw0[1] = 0
    ind2T = np.zeros((2, 128), np.float32)
    ind2T[0, 0:64] = 1; ind2T[1, 64:128] = 1
    return {
        "wqkv": Wqkv, "woB": Wo,
        "qs128": np.tile(qs, 2)[:, None].copy(),
        "qs128sw": np.tile(np.concatenate([qs[32:], qs[:32]]), 2)[:, None].copy(),
        "ks128": np.tile(ks, 2)[:, None].copy(),
        "ks128sw": np.tile(np.concatenate([ks[32:], ks[:32]]), 2)[:, None].copy(),
        "qsrow": qs[None, :].copy(), "ksrow": ks[None, :].copy(),
        "p32": host_p32(), "ind2T": ind2T, "identf": host_identity(),
        "_mstd": mstd, "_mw0": mw0,
    }

def _battn_in_maps(shared, xx_full, rope_cache):
    in_maps = []
    for c in range(NCORES):
        t0 = c * TOWN
        if t0 not in rope_cache:
            Cq, Sq, Ck, Sk = host_tables_B(t0)
            rope_cache[t0] = (Cq.astype(np.float32), Sq.astype(np.float32),
                             Ck.astype(np.float32), Sk.astype(np.float32))
        Cq, Sq, Ck, Sk = rope_cache[t0]
        halo = np.zeros((B, D, TOWN + 257), np.float32)
        lo = t0 - 257
        if lo < 0:
            halo[:, :, 257:] = xx_full[:, :, t0:t0 + TOWN]
        else:
            halo[:, :, :] = xx_full[:, :, lo:t0 + TOWN]
        m = {k: v for k, v in shared.items() if not k.startswith("_")}
        m["xxT"] = halo
        m["tabCq"], m["tabSq"], m["tabCk"], m["tabSk"] = Cq, Sq, Ck, Sk
        m["masksB"] = np.stack([shared["_mstd"],
                                shared["_mw0"] if c == 0 else shared["_mstd"]])
        in_maps.append(m)
    return in_maps

def host_prep_Bffn_shared(I, layer):
    W1 = I["L0_ffn_W1"][layer].astype(np.float32)
    b1v = I["L0_ffn_b1"][layer].astype(np.float32)
    W2 = I["L0_ffn_W2"][layer].astype(np.float32)
    b2v = I["L0_ffn_b2"][layer].astype(np.float32)
    g = I["L0_ffn_ln_g"][layer].astype(np.float32)
    bl = I["L0_ffn_ln_b"][layer].astype(np.float32)
    W1p = np.zeros((D, 5632), np.float32)
    W1p[:, 0:2730] = W1[:, 0:2730]
    W1p[:, 2816:2816 + 2730] = W1[:, 2730:]
    b1p = np.zeros(5632, np.float32)
    b1p[0:2730] = b1v[0:2730]; b1p[2816:2816 + 2730] = b1v[2730:]
    W2p = np.zeros((2816, D), np.float32); W2p[0:2730] = W2
    return {
        "w1p": W1p, "w2p": W2p,
        "b1p": np.ascontiguousarray(b1p.reshape(44, 128).T),
        "b2c": np.ascontiguousarray(b2v.reshape(8, 128).T),
        "gc": np.ascontiguousarray(g.reshape(8, 128).T),
        "blnc": np.ascontiguousarray(bl.reshape(8, 128).T),
    }

def _bffn_in_maps(shared, xx_full):
    in_maps = []
    for c in range(NCORES):
        t0 = c * TOWN
        halo = np.zeros((B, D, TOWN + 1), np.float32)
        if t0 == 0:
            halo[:, :, 1:] = xx_full[:, :, 0:TOWN]
        else:
            halo[:, :, :] = xx_full[:, :, t0 - 1:t0 + TOWN]
        m = dict(shared)
        m["xxT"] = halo
        in_maps.append(m)
    return in_maps

def _reassemble(results, key):
    return np.concatenate([r[key] for r in results], axis=2)


def kernel(**inputs):
    I = inputs
    x = np.asarray(I["L0_attn_x"], dtype=np.float32)
    xTfull = np.ascontiguousarray(x.transpose(0, 2, 1))
    cores = list(range(NCORES))

    nc = _get("A1", build_A1)
    r1 = run_bass_kernel_spmd(nc, host_prep_A1(I), core_ids=cores).results

    kT_all = np.concatenate([r["kT"] for r in r1], axis=2)
    v_all = np.concatenate([r["vtok"] for r in r1], axis=3)
    nrm_all = np.stack([r["nrm"] for r in r1])
    sel2 = np.eye(2, dtype=np.float32)
    in_maps = []
    for c in cores:
        in_maps.append({"qTd": r1[c]["qT"], "kTd": kT_all, "vAd": v_all,
                        "nrmO": nrm_all[c], "nrmA": nrm_all, "sel2": sel2})
    nc = _get("A2", build_A2)
    r2 = run_bass_kernel_spmd(nc, in_maps, core_ids=cores).results

    woT = np.ascontiguousarray(I["L0_attn_Wo"].astype(np.float32).T)
    in_maps = []
    for c in cores:
        t0 = c * TOWN
        in_maps.append({"xT": np.ascontiguousarray(xTfull[:, :, t0:t0 + TOWN]),
                        "aoT": r2[c]["attnO"], "woT": woT})
    nc = _get("A3", build_A3)
    r3 = run_bass_kernel_spmd(nc, in_maps, core_ids=cores).results
    x1_full = _reassemble(r3, "x1T")
    xx_full = _reassemble(r3, "xx0T")

    rope_cache = {}
    nca = _get("Battn", build_Battn)
    ncf = _get("Bffn", build_Bffn)
    for layer in range(6):
        sha = host_prep_Battn_shared(I, layer)
        ra = run_bass_kernel_spmd(nca, _battn_in_maps(sha, xx_full, rope_cache),
                                  core_ids=cores).results
        xx_full = _reassemble(ra, "xxoT")
        shf = host_prep_Bffn_shared(I, layer)
        rf = run_bass_kernel_spmd(ncf, _bffn_in_maps(shf, xx_full),
                                  core_ids=cores).results
        xx_full = _reassemble(rf, "xxoT")

    lb = I["L1_attn_lower_bound"].astype(np.float32)
    shared = {
        "winp": np.ascontiguousarray(I["L1_attn_W_in"].astype(np.float32)),
        "woutp": np.ascontiguousarray(I["L1_attn_W_out"].astype(np.float32)),
        "lbc": np.ascontiguousarray(lb.reshape(8, 128).T),
        "pairE": host_pairE(),
    }
    in_maps = []
    for c in cores:
        t0 = c * TOWN
        m = dict(shared)
        m["x1T"] = np.ascontiguousarray(x1_full[:, :, t0:t0 + TOWN])
        m["xxT"] = np.ascontiguousarray(xx_full[:, :, t0:t0 + TOWN])
        in_maps.append(m)
    nc = _get("BC", build_BC)
    rc = run_bass_kernel_spmd(nc, in_maps, core_ids=cores).results
    x3_full = _reassemble(rc, "x3T")

    tr = lambda a: np.ascontiguousarray(I[a].astype(np.float32).T)
    cm = lambda a: np.ascontiguousarray(I[a].astype(np.float32).reshape(8, 128).T)
    shared = {
        "wfT": tr("L1_ffn_wf"), "ufT": tr("L1_ffn_uf"),
        "wiT": tr("L1_ffn_wi"), "uiT": tr("L1_ffn_ui"),
        "wcT": tr("L1_ffn_wc"), "ucT": tr("L1_ffn_uc"),
        "woT2": tr("L1_ffn_wo"), "uoT": tr("L1_ffn_uo"),
        "bfc": cm("L1_ffn_bf"), "bic": cm("L1_ffn_bi"),
        "bcc": cm("L1_ffn_bc"), "boc": cm("L1_ffn_bo"),
        "incc": cm("L1_ffn_in_c"),
    }
    in_maps = []
    for c in cores:
        t0 = c * TOWN
        m = dict(shared)
        m["x3T"] = np.ascontiguousarray(x3_full[:, :, t0:t0 + TOWN])
        in_maps.append(m)
    nc = _get("D", build_D)
    rd = run_bass_kernel_spmd(nc, in_maps, core_ids=cores).results
    x4_full = _reassemble(rd, "x4T")

    return np.ascontiguousarray(x4_full.transpose(0, 2, 1))
